# version 03
# baseline (speedup 1.0000x reference)
import numpy as np
import ml_dtypes
import concourse.bass as bass
import concourse.mybir as mybir
from concourse.bass_utils import run_bass_kernel_spmd

F32 = mybir.dt.float32
BF16 = mybir.dt.float16
AF = mybir.ActivationFunctionType
ALU = mybir.AluOpType

B, S, D, H, DH, MB = 4, 4096, 2048, 16, 128, 16
TOK = 2048
NSTRIP = TOK // 128
NC = D // 128
NBLK = S // 128
INV_N = 2.0 / (MB * DH)
LN_EPS_OUTER = 1e-5
LN_EPS_A = 1e-5
LN_EPS_F = 1e-6
HID_SLICE = 8192 // 8

DEBUG_PHASES = None
FFN_STAGE = 9
ZLN_STAGE = 9
ATTN_A_STAGE = 9


class SemBank:
    def __init__(self):
        self.handles = {}
        self.count = {}

    def add(self, name, h):
        self.handles[name] = h
        self.count[name] = 0

    def inc(self, name, n):
        self.count[name] += n
        return self.handles[name]

    def val(self, name):
        return self.count[name]

    def h(self, name):
        return self.handles[name]


class Prog:
    def __init__(self, nc, block, sems):
        self.nc = nc
        self.block = block
        self.s = sems

    def sec(self, engname, fn):
        getattr(self.block, engname)(fn)

    def ld_dma(self, eng, out, in_, **kw):
        eng.wait_ge(self.s.h("LD"), self.s.val("LD"))
        return eng.dma_start(out=out, in_=in_, **kw).then_inc(self.s.inc("LD", 16), 16)

    def switch_sec(self, engname, body_fn):
        def fn(eng):
            for arm in eng.switch_core_id(n=128):
                body_fn(eng, arm.logical % 2)
        self.sec(engname, fn)


CI_IDENT, CI_CAUSAL, CI_ROT, CI_ONES = 0, 1, 2, 3
CI_MUN = 4
CI_MLN = 8
CI_MONES = 12
CI_CADD = 13

SCR_W0 = 0
SCR_STG = 2048
SCR_SP = 4096
SCN_W = 0
SCN_NT = 1024
SCN_NN = 1536
SCN_N2T = 2048
SCN_N2 = 2560
SCN_N4T = 3072
SCN_B = 3584
SCN_G = 4096
SCN_KH = 4608


def build_program():
    nc = bass.Bass("TRN2", target_bir_lowering=False)

    dt = {}
    def din(name, shape, dtype=F32):
        dt[name] = nc.dram_tensor(name, list(shape), dtype, kind="ExternalInput").ap()
        return dt[name]

    din("x_own", [TOK, D])
    y_out = nc.dram_tensor("y_own", [TOK, D], F32, kind="ExternalOutput").ap()

    din("WkT", [D, D]); din("WvT", [D, D]); din("WoT", [D, D])
    din("ln_g", [D]); din("ln_b", [D])
    din("WQT", [D, D]); din("WKT", [D, D]); din("WVT", [D, D]); din("WOT", [D, D])
    din("agg_w", [D, D]); din("agg_b", [D])
    din("pn_g", [D]); din("pn_b", [D])
    din("d1_w", [D, HID_SLICE]); din("d1_b", [HID_SLICE])
    din("d2_w", [HID_SLICE, D]); din("d2_b", [D])
    din("bc_w", [D, D]); din("bc_b", [D])
    din("gate_w", [D, D]); din("gate_b", [D])
    din("out_g", [D]); din("out_b", [D])
    din("in_g", [D]); din("in_b", [D])
    din("rope_cos", [DH, TOK], F32)
    din("rope_sin", [DH, TOK], F32)
    din("consts", [16, 128, 128], F32)

    sh_kv = nc.dram_tensor("sh_kv", [2, 2, H, DH, TOK], BF16, kind="Internal",
                           addr_space="Shared").ap()
    sh_za = nc.dram_tensor("sh_za", [H, NBLK, 128, DH], BF16, kind="Internal",
                           addr_space="Shared").ap()
    sh_qkv = nc.dram_tensor("sh_qkv", [2, 3, H, DH, TOK], BF16, kind="Internal",
                            addr_space="Shared").ap()
    sh_zb = nc.dram_tensor("sh_zb", [H, NBLK, 128, DH], BF16, kind="Internal",
                           addr_space="Shared").ap()
    cc_bar_i = nc.dram_tensor("cc_bar_i", [1, 8], F32, kind="Internal").ap()
    cc_bar_o = nc.dram_tensor("cc_bar_o", [1, 8], F32, kind="Internal").ap()
    cc_row_i = nc.dram_tensor("cc_row_i", [1, D], F32, kind="Internal").ap()
    cc_row_o = nc.dram_tensor("cc_row_o", [8, D], F32, kind="Internal").ap()
    cc_th_i = nc.dram_tensor("cc_th_i", [B, D], F32, kind="Internal").ap()
    cc_th_o = nc.dram_tensor("cc_th_o", [B, D], F32, kind="Internal").ap()

    sems = SemBank()
    with (
        nc.sbuf_tensor("X", [128, NSTRIP, D], BF16) as X,
        nc.sbuf_tensor("P1", [128, 16, 2048], BF16) as P1,
        nc.sbuf_tensor("P2", [128, 16, 2048], BF16) as P2,
        nc.sbuf_tensor("SCR", [128, 5120], BF16) as SCR,
        nc.sbuf_tensor("CONST", [128, 16, 128], BF16) as CONST,
        nc.sbuf_tensor("SMF", [128, 448], F32) as SMF,
        nc.semaphore("LD") as _ld,
        nc.semaphore("TPS") as _tps,
        nc.semaphore("EXC") as _exc,
        nc.semaphore("ST") as _st,
        nc.semaphore("CC") as _cc,
        nc.semaphore("TE") as _te,
        nc.semaphore("VE") as _ve,
        nc.semaphore("SE") as _se,
        nc.semaphore("GE") as _ge,
        nc.Block() as block,
    ):
        for nm, hh in [("LD", _ld), ("TPS", _tps), ("EXC", _exc), ("ST", _st),
                       ("CC", _cc), ("TE", _te), ("VE", _ve), ("SE", _se), ("GE", _ge)]:
            sems.add(nm, hh)

        p = Prog(nc, block, sems)
        p.dt = dt
        p.X, p.P1, p.P2, p.SCR, p.CONST, p.SMF = X, P1, P2, SCR, CONST, SMF
        p.y_out = y_out
        p.sh = dict(kv=sh_kv, za=sh_za, qkv=sh_qkv, zb=sh_zb)
        p.cc = dict(bar_i=cc_bar_i, bar_o=cc_bar_o, row_i=cc_row_i, row_o=cc_row_o,
                    th_i=cc_th_i, th_o=cc_th_o)
        p.RG8 = [list(range(8))]

        phases = [
            ph_load_input,
            ph_ln_outer,
            ph_attn_a,
            lambda p: ph_ffn(p, 1),
            ph_attn_b,
            lambda p: ph_ffn(p, 2),
        ]
        n = len(phases) if DEBUG_PHASES is None else DEBUG_PHASES
        for ph in phases[:n]:
            ph(p)
        ph_store_x(p)
        ph_final_wait(p)

    return nc



def ph_load_input(p):
    s = p.s
    dt = p.dt

    def g(eng):
        p.ld_dma(eng, p.X[:], dt["x_own"].rearrange("(s p) d -> p s d", p=128))
        p.ld_dma(eng, p.CONST[:], dt["consts"].rearrange("c p f -> p c f"))
        eng.memset(p.SMF[:, 440:448], 0.0).then_inc(s.inc("GE", 1), 1)
        eng.wait_ge(s.h("GE"), s.count["GE"])
        p.ld_dma(eng, p.cc["bar_i"], p.SMF[0:1, 440:448])
    p.sec("gpsimd", g)
    p.ld_x = s.val("LD")


def emit_ln_plain(p, src_ap_fn, dst_ap_fn, eps, wait=None):
    s = p.s
    stats = p.SMF

    def v(eng):
        if wait:
            wait(eng)
        for i in range(NSTRIP):
            src = src_ap_fn(i)
            if i:
                eng.wait_ge(s.h("VE"), s.count["VE"])
            for q in range(4):
                op = eng.bn_stats(stats[:, 32 + 6 * q:32 + 6 * q + 6],
                                  src[:, q * 512:(q + 1) * 512])
            op.then_inc(s.inc("VE", 1), 1)
            eng.wait_ge(s.h("VE"), s.count["VE"])
            eng.bn_aggr(stats[:, 2 * i:2 * i + 2],
                        stats[:, 32:56]).then_inc(s.inc("VE", 1), 1)
        eng.wait_ge(s.h("VE"), s.count["VE"])
        eng.tensor_scalar(stats[:, 128:144], stats[:, 1:32:2], eps, None,
                          ALU.add).then_inc(s.inc("VE", 1), 1)
    p.sec("vector", v)
    t_var = s.val("VE")

    def sc(eng):
        eng.wait_ge(s.h("VE"), t_var)
        eng.activation(stats[:, 144:160], stats[:, 128:144],
                       AF.Sqrt).then_inc(s.inc("SE", 1), 1)
    p.sec("scalar", sc)
    t_sqrt = s.val("SE")

    def v2(eng):
        eng.wait_ge(s.h("SE"), t_sqrt)
        eng.reciprocal(stats[:, 160:176],
                       stats[:, 144:160]).then_inc(s.inc("VE", 1), 1)
        eng.wait_ge(s.h("VE"), s.count["VE"])
        for i in range(NSTRIP):
            eng.tensor_scalar(dst_ap_fn(i), src_ap_fn(i),
                              stats[:, 2 * i:2 * i + 1],
                              stats[:, 160 + i:160 + i + 1],
                              ALU.subtract, ALU.mult).then_inc(s.inc("VE", 1), 1)
    p.sec("vector", v2)
    return s.val("VE")


def emit_transpose16(p, src_ap_fn, dst_lo, dst_hi, wait_ve):
    s = p.s

    def sy(eng):
        eng.wait_ge(s.h("VE"), wait_ve)
        for i in range(NSTRIP):
            src = src_ap_fn(i)
            eng.dma_start(out=dst_lo(i), in_=src[:, 0:1024], transpose=True
                          ).then_inc(s.inc("TPS", 16), 16)
            eng.dma_start(out=dst_hi(i), in_=src[:, 1024:2048], transpose=True
                          ).then_inc(s.inc("TPS", 16), 16)
    p.sec("sync", sy)
    return s.val("TPS")


def ph_ln_outer(p):
    s = p.s

    def w(eng):
        eng.wait_ge(s.h("LD"), p.ld_x)
    t = emit_ln_plain(p, lambda i: p.X[:, i, :], lambda i: p.P2[:, i, :],
                      LN_EPS_OUTER, wait=w)
    p.t_lnT = emit_transpose16(
        p, lambda i: p.P2[:, i, :],
        lambda i: p.P1[:, 0:8, i * 128:(i + 1) * 128],
        lambda i: p.P1[:, 8:16, i * 128:(i + 1) * 128], t)


def emit_barrier(p):
    s = p.s
    exc_t, tps_t = s.val("EXC"), s.val("TPS")

    def g(eng):
        eng.wait_ge(s.h("EXC"), exc_t)
        eng.wait_ge(s.h("TPS"), tps_t)
        eng.wait_ge(s.h("LD"), 48)
        eng.collective_compute("AllReduce", ALU.add, replica_groups=p.RG8,
                               ins=[p.cc["bar_i"]], outs=[p.cc["bar_o"]]
                               ).then_inc(s.inc("CC", 1), 1)
    p.sec("gpsimd", g)
    return s.val("CC")


def emit_proj_to_shared(p, wname, lnfm, kvslot, shared, rope, psA, psB,
                        extra_wait_ld=None):
    s = p.s
    wsl = [p.SCR[:, SCR_W0:SCR_W0 + 2048].rearrange("p (c f) -> p c f", f=128),
           p.P2[:, 12, :].rearrange("p (c f) -> p c f", f=128)]
    stg = p.SCR[:, SCR_STG:SCR_STG + 2048]
    rot_tmp = p.P2[:, 10, :]
    ps = [psA, psB]

    ld0 = s.val("LD")
    te0 = s.val("TE")
    se0 = s.val("SE")
    ve0 = s.val("VE")
    exc0 = s.val("EXC")
    per_fc_te = 2 if rope else 1
    mm_done = {fc: te0 + fc * per_fc_te + 1 for fc in range(NC)}
    rot_done = {fc: te0 + fc * per_fc_te + 2 for fc in range(NC)}
    evac_done = {fc: se0 + fc + 1 for fc in range(NC)}
    rope_done = {fc: ve0 + (fc + 1) * 3 for fc in range(NC)}
    route_done = {fc: exc0 + (fc + 1) * 16 for fc in range(NC)}

    def g2(eng):
        for fc in range(NC):
            if fc >= 2:
                eng.wait_ge(s.h("TE"), mm_done[fc - 2])
            else:
                eng.wait_ge(s.h("TPS"), p.t_lnT)
                if te0 > 0:
                    eng.wait_ge(s.h("TE"), te0)
                if ve0 > 0:
                    eng.wait_ge(s.h("VE"), ve0)
                if se0 > 0:
                    eng.wait_ge(s.h("SE"), se0)
            p.ld_dma(eng, wsl[fc % 2][:],
                     p.dt[wname][:, fc * 128:(fc + 1) * 128].rearrange(
                         "(c p) f -> p c f", p=128))
    p.sec("gpsimd", g2)

    def te(eng):
        for fc in range(NC):
            eng.wait_ge(s.h("LD"), ld0 + (fc + 1) * 16)
            if fc == 0:
                eng.wait_ge(s.h("TPS"), p.t_lnT)
                eng.wait_ge(s.h("VE"), ve0)
                eng.wait_ge(s.h("SE"), se0)
                eng.wait_ge(s.h("EXC"), exc0)
                if extra_wait_ld is not None:
                    eng.wait_ge(s.h("LD"), extra_wait_ld)
            if fc >= 2:
                eng.wait_ge(s.h("SE"), evac_done[fc - 2])
                if rope:
                    eng.wait_ge(s.h("VE"), rope_done[fc - 2])
            w = wsl[fc % 2]
            pp = ps[fc % 2]
            for tq in range(4):
                for gc in range(NC):
                    mm = eng.matmul(pp[:, tq * 512:(tq + 1) * 512], w[:, gc, :],
                                    lnfm(gc)[:, tq * 512:(tq + 1) * 512],
                                    start=(gc == 0), stop=(gc == NC - 1))
            mm.then_inc(s.inc("TE", 1), 1)
            if rope:
                eng.wait_ge(s.h("SE"), evac_done[fc])
                for tq in range(4):
                    mm = eng.matmul(pp[:, tq * 512:(tq + 1) * 512],
                                    p.CONST[:, CI_ROT, :],
                                    stg[:, tq * 512:(tq + 1) * 512],
                                    start=True, stop=True)
                mm.then_inc(s.inc("TE", 1), 1)
    p.sec("tensor", te)

    def sc(eng):
        for fc in range(NC):
            eng.wait_ge(s.h("TE"), mm_done[fc])
            if rope and fc >= 1:
                eng.wait_ge(s.h("VE"), rope_done[fc - 1])
            if not rope and fc >= 1:
                eng.wait_ge(s.h("EXC"), route_done[fc - 1])
            eng.activation(stg[:], ps[fc % 2][:], AF.Copy).then_inc(s.inc("SE", 1), 1)
    p.sec("scalar", sc)

    if rope:
        outbox = [p.P2[:, 11, :], p.P2[:, 13, :]]

        def ve(eng):
            for fc in range(NC):
                eng.wait_ge(s.h("TE"), rot_done[fc])
                if fc >= 2:
                    eng.wait_ge(s.h("EXC"), route_done[fc - 2])
                eng.tensor_tensor(rot_tmp[:], ps[fc % 2][:], p.P2[:, 9, :], ALU.mult)
                eng.tensor_tensor(stg[:], stg[:], p.P2[:, 8, :],
                                  ALU.mult).then_inc(s.inc("VE", 2), 2)
                eng.wait_ge(s.h("VE"), s.count["VE"])
                eng.tensor_tensor(outbox[fc % 2][:], stg[:], rot_tmp[:],
                                  ALU.add).then_inc(s.inc("VE", 1), 1)
        p.sec("vector", ve)
        src_of = lambda fc: outbox[fc % 2]
        wait_tick = lambda fc: ("VE", rope_done[fc])
    else:
        src_of = lambda fc: stg
        wait_tick = lambda fc: ("SE", evac_done[fc])

    def route(eng, parity):
        ex0 = exc0
        for fc in range(NC):
            nm, tk = wait_tick(fc)
            eng.wait_ge(s.h(nm), tk)
            eng.wait_ge(s.h("EXC"), ex0 + 16 * fc)
            eng.dma_start(out=shared[parity, kvslot, fc, :, :], in_=src_of(fc)
                          ).then_inc(s.h("EXC"), 16)
    p.switch_sec("sync", route)
    s.inc("EXC", 16 * NC)


def kT_ap(p, h, Jg):
    ar = p.P1 if Jg < 16 else p.P2
    j = Jg % 16
    return ar[:, h, j * 128:(j + 1) * 128]


def vh_ap(p, h, Jg):
    ar = p.P1 if Jg < 16 else p.P2
    j = Jg % 16
    return ar[:, 8 + h, j * 128:(j + 1) * 128]


def zown_ap(p, i):
    return (p.P1 if i < 8 else p.P2)[:, 8 + (i % 8), :]


def lnzfm_ap(p, gc):
    return (p.P1 if gc < 8 else p.P2)[:, gc % 8, :]


def wores_ap(p, gc):
    return (p.P1 if gc < 8 else p.P2)[:, 8 + (gc % 8), :]


def ph_attn_a(p):
    s = p.s
    nc = p.nc

    def g(eng):
        eng.wait_ge(s.h("TPS"), p.t_lnT)
        p.ld_dma(eng, p.P2[:, 8, :], p.dt["rope_cos"])
        p.ld_dma(eng, p.P2[:, 9, :], p.dt["rope_sin"])
    p.sec("gpsimd", g)
    ld_rope = s.val("LD")

    lnfm = lambda gc: p.P1[:, gc, :]

    with (
        nc.psum_tensor("psA", [128, 2048], F32) as psA,
        nc.psum_tensor("psB", [128, 2048], F32) as psB,
    ):
        emit_proj_to_shared(p, "WkT", lnfm, 0, p.sh["kv"], True, psA, psB,
                            extra_wait_ld=ld_rope)
        if ATTN_A_STAGE < 2:
            return
        emit_proj_to_shared(p, "WvT", lnfm, 1, p.sh["kv"], False, psA, psB)

    if ATTN_A_STAGE < 3:
        return
    bar1 = emit_barrier(p)
    if ATTN_A_STAGE < 4:
        return

    def reads(eng, parity):
        eng.wait_ge(s.h("CC"), bar1)
        for h in range(8):
            hg = 8 * parity + h
            eng.dma_start(out=p.P1[:, h, :], in_=p.sh["kv"][0, 0, hg, :, :]
                          ).then_inc(s.h("EXC"), 16)
            eng.dma_start(out=p.P2[:, h, :], in_=p.sh["kv"][1, 0, hg, :, :]
                          ).then_inc(s.h("EXC"), 16)
            eng.dma_start(out=p.P1[:, 8 + h, :].rearrange("p (c f) -> p c f", f=128),
                          in_=p.sh["kv"][0, 1, hg, :, :], transpose=True
                          ).then_inc(s.h("EXC"), 16)
            eng.dma_start(out=p.P2[:, 8 + h, :].rearrange("p (c f) -> p c f", f=128),
                          in_=p.sh["kv"][1, 1, hg, :, :], transpose=True
                          ).then_inc(s.h("EXC"), 16)
    p.switch_sec("sync", reads)
    s.inc("EXC", 16 * 32)
    t_reads = s.val("EXC")
    if ATTN_A_STAGE < 5:
        return

    def v0(eng):
        eng.wait_ge(s.h("EXC"), t_reads)
        for h in range(8):
            op = eng.tensor_copy(p.SCR[:, SCN_W + h * 128:SCN_W + (h + 1) * 128],
                                 p.CONST[:, CI_IDENT, :])
        op.then_inc(s.inc("VE", 1), 1)
    p.sec("vector", v0)
    t_prep = s.val("VE")
    if ATTN_A_STAGE < 6:
        return

    emit_scan(p, t_prep)
    if ATTN_A_STAGE < 7:
        return
    emit_z_collect(p)
    if ATTN_A_STAGE < 8:
        return
    emit_zln_wo(p)


def emit_scan(p, t_prep):
    s = p.s
    nc = p.nc
    p._scn_v_done = None

    with (
        nc.psum_tensor("sA", [128, 512], F32) as sA,
        nc.psum_tensor("sBase", [128, 512], F32) as sBase,
        nc.psum_tensor("sAp1", [128, 512], F32) as sAp1,
        nc.psum_tensor("sAp2", [128, 512], F32) as sAp2,
        nc.psum_tensor("sDW", [128, 512], F32) as sDW,
        nc.psum_tensor("sKH", [128, 512], F32) as sKH,
    ):
        first = True
        for Jg in range(NBLK):
            for grp in range(2):
                hs = [4 * grp + i for i in range(4)]
                emit_scan_block_group(p, Jg, grp, hs, sA, sBase, sAp1, sAp2,
                                      sDW, sKH, t_prep if first else None)
                first = False


def emit_scan_block_group(p, Jg, grp, hs, sA, sBase, sAp1, sAp2, sDW, sKH, t_prep):
    s = p.s
    SC = p.SCR
    c512 = lambda base: SC[:, base:base + 512]
    c128 = lambda base, i: SC[:, base + i * 128:base + (i + 1) * 128]
    prev_v = p._scn_v_done

    def te1(eng):
        if t_prep is not None:
            eng.wait_ge(s.h("VE"), t_prep)
        if prev_v is not None:
            eng.wait_ge(s.h("VE"), prev_v)
        for i, h in enumerate(hs):
            mm = eng.matmul(sA[:, i * 128:(i + 1) * 128], kT_ap(p, h, Jg),
                            kT_ap(p, h, Jg), start=True, stop=True)
        mm.then_inc(s.inc("TE", 1), 1)
        for i, h in enumerate(hs):
            mm = eng.matmul(sBase[:, i * 128:(i + 1) * 128], kT_ap(p, h, Jg),
                            SC[:, SCN_W + h * 128:SCN_W + (h + 1) * 128],
                            start=True, stop=True)
        mm.then_inc(s.inc("TE", 1), 1)
    p.sec("tensor", te1)
    tA = s.val("TE") - 1
    tBase = s.val("TE")

    def v1(eng):
        if t_prep is not None:
            eng.wait_ge(s.h("VE"), t_prep)
        eng.wait_ge(s.h("TE"), tA)
        eng.tensor_tensor(c512(SCN_NT), sA[:],
                          p.CONST[:, CI_MUN:CI_MUN + 4, :], ALU.mult)
        eng.tensor_tensor(c512(SCN_NN), sA[:],
                          p.CONST[:, CI_MLN:CI_MLN + 4, :], ALU.mult
                          ).then_inc(s.inc("VE", 2), 2)
        eng.wait_ge(s.h("TE"), tBase)
        eng.tensor_tensor(c512(SCN_B), sBase[:], _vh4(p, hs, Jg),
                          ALU.subtract).then_inc(s.inc("VE", 1), 1)
    p.sec("vector", v1)
    tMask = s.val("VE") - 1
    tB = s.val("VE")

    def te2(eng):
        eng.wait_ge(s.h("VE"), tMask)
        for i in range(4):
            mm = eng.matmul(sA[:, i * 128:(i + 1) * 128], c128(SCN_NN, i),
                            c128(SCN_NT, i), start=True, stop=True)
        mm.then_inc(s.inc("TE", 1), 1)
    p.sec("tensor", te2)
    tN2T = s.val("TE")

    def sc1(eng):
        eng.wait_ge(s.h("TE"), tN2T)
        eng.activation(c512(SCN_N2T), sA[:], AF.Copy).then_inc(s.inc("SE", 1), 1)
    p.sec("scalar", sc1)
    tN2Tev = s.val("SE")

    def te3(eng):
        eng.wait_ge(s.h("SE"), tN2Tev)
        for i in range(4):
            mm = eng.matmul(sA[:, i * 128:(i + 1) * 128], c128(SCN_NT, i),
                            c128(SCN_NN, i), start=True, stop=True)
        mm.then_inc(s.inc("TE", 1), 1)
    p.sec("tensor", te3)
    tN2 = s.val("TE")

    def sc2(eng):
        eng.wait_ge(s.h("TE"), tN2)
        eng.activation(c512(SCN_N2), sA[:], AF.Copy).then_inc(s.inc("SE", 1), 1)
    p.sec("scalar", sc2)
    tN2ev = s.val("SE")

    def te4(eng):
        eng.wait_ge(s.h("SE"), tN2ev)
        for i in range(4):
            mm = eng.matmul(sA[:, i * 128:(i + 1) * 128], c128(SCN_N2, i),
                            c128(SCN_N2T, i), start=True, stop=True)
        mm.then_inc(s.inc("TE", 1), 1)
    p.sec("tensor", te4)
    tN4 = s.val("TE")

    def sc3(eng):
        eng.wait_ge(s.h("TE"), tN4)
        eng.activation(c512(SCN_N4T), sA[:], AF.Copy).then_inc(s.inc("SE", 1), 1)
    p.sec("scalar", sc3)
    tN4ev = s.val("SE")

    def te5(eng):
        eng.wait_ge(s.h("VE"), tB)
        for i in range(4):
            eng.matmul(sAp1[:, i * 128:(i + 1) * 128], p.CONST[:, CI_IDENT, :],
                       c128(SCN_B, i), start=True, stop=False)
            mm = eng.matmul(sAp1[:, i * 128:(i + 1) * 128], c128(SCN_NT, i),
                            c128(SCN_B, i), start=False, stop=True)
        mm.then_inc(s.inc("TE", 1), 1)
    p.sec("tensor", te5)
    tY1 = s.val("TE")

    def sc4(eng):
        eng.wait_ge(s.h("TE"), tY1)
        eng.activation(c512(SCN_B), sAp1[:], AF.Copy).then_inc(s.inc("SE", 1), 1)
    p.sec("scalar", sc4)
    tY1ev = s.val("SE")

    def te6(eng):
        eng.wait_ge(s.h("SE"), tY1ev)
        for i in range(4):
            eng.matmul(sAp2[:, i * 128:(i + 1) * 128], p.CONST[:, CI_IDENT, :],
                       c128(SCN_B, i), start=True, stop=False)
            mm = eng.matmul(sAp2[:, i * 128:(i + 1) * 128], c128(SCN_N2T, i),
                            c128(SCN_B, i), start=False, stop=True)
        mm.then_inc(s.inc("TE", 1), 1)
    p.sec("tensor", te6)
    tY2 = s.val("TE")

    def sc5(eng):
        eng.wait_ge(s.h("TE"), tY2)
        eng.activation(c512(SCN_B), sAp2[:], AF.Copy).then_inc(s.inc("SE", 1), 1)
    p.sec("scalar", sc5)
    tY2ev = s.val("SE")

    def te7(eng):
        eng.wait_ge(s.h("SE"), tY2ev)
        eng.wait_ge(s.h("SE"), tN4ev)
        for i in range(4):
            eng.matmul(sAp1[:, i * 128:(i + 1) * 128], p.CONST[:, CI_IDENT, :],
                       c128(SCN_B, i), start=True, stop=False)
            mm = eng.matmul(sAp1[:, i * 128:(i + 1) * 128], c128(SCN_N4T, i),
                            c128(SCN_B, i), start=False, stop=True)
        mm.then_inc(s.inc("TE", 1), 1)
        for i, h in enumerate(hs):
            mm = eng.matmul(sKH[:, i * 128:(i + 1) * 128], kT_ap(p, h, Jg),
                            p.CONST[:, CI_IDENT, :], start=True, stop=True)
        mm.then_inc(s.inc("TE", 1), 1)
    p.sec("tensor", te7)
    tG = s.val("TE") - 1
    tKH = s.val("TE")

    def sc6(eng):
        eng.wait_ge(s.h("TE"), tG)
        eng.activation(c512(SCN_G), sAp1[:], AF.Copy).then_inc(s.inc("SE", 1), 1)
        eng.wait_ge(s.h("TE"), tKH)
        eng.activation(c512(SCN_KH), sKH[:], AF.Copy).then_inc(s.inc("SE", 1), 1)
    p.sec("scalar", sc6)
    tGev = s.val("SE") - 1
    tKHev = s.val("SE")

    def te8(eng):
        eng.wait_ge(s.h("SE"), tKHev)
        for i, h in enumerate(hs):
            mm = eng.matmul(sDW[:, i * 128:(i + 1) * 128], c128(SCN_KH, i),
                            c128(SCN_G, i), start=True, stop=True)
        mm.then_inc(s.inc("TE", 1), 1)
    p.sec("tensor", te8)
    tDW = s.val("TE")

    def v2(eng):
        eng.wait_ge(s.h("SE"), tGev)
        eng.wait_ge(s.h("TE"), tDW)
        for i, h in enumerate(hs):
            op = eng.tensor_tensor(kT_ap(p, h, Jg), c128(SCN_G, i), vh_ap(p, h, Jg),
                                   ALU.add)
        op.then_inc(s.inc("VE", 4), 4)
        eng.scalar_tensor_tensor(SC[:, SCN_W + grp * 512:SCN_W + (grp + 1) * 512],
                                 sDW[:], -INV_N,
                                 SC[:, SCN_W + grp * 512:SCN_W + (grp + 1) * 512],
                                 ALU.mult, ALU.add).then_inc(s.inc("VE", 1), 1)
    p.sec("vector", v2)
    p._scn_v_done = s.val("VE")


def _vh4(p, hs, Jg):
    ar = p.P1 if Jg < 16 else p.P2
    j = Jg % 16
    h0 = hs[0]
    return ar[:, 8 + h0:8 + h0 + 4, j * 128:(j + 1) * 128]


def emit_z_collect(p):
    s = p.s
    v_done = p._scn_v_done

    def wr(eng, parity):
        eng.wait_ge(s.h("VE"), v_done)
        for h in range(8):
            hg = 8 * parity + h
            eng.dma_start(out=p.sh["za"][hg, 0:16].rearrange("j t e -> t j e"),
                          in_=p.P1[:, h, :].rearrange("p (j e) -> p j e", e=128)
                          ).then_inc(s.h("EXC"), 16)
            eng.dma_start(out=p.sh["za"][hg, 16:32].rearrange("j t e -> t j e"),
                          in_=p.P2[:, h, :].rearrange("p (j e) -> p j e", e=128)
                          ).then_inc(s.h("EXC"), 16)
    p.switch_sec("sync", wr)
    s.inc("EXC", 16 * 16)

    bar = emit_barrier(p)

    def rd(eng, parity):
        eng.wait_ge(s.h("CC"), bar)
        J0 = 16 * parity
        for hg in range(16):
            eng.dma_start(
                out=p.P1[:, 8:16, hg * 128:(hg + 1) * 128],
                in_=p.sh["za"][hg, J0:J0 + 8].rearrange("j t e -> t j e")
                ).then_inc(s.h("EXC"), 16)
            eng.dma_start(
                out=p.P2[:, 8:16, hg * 128:(hg + 1) * 128],
                in_=p.sh["za"][hg, J0 + 8:J0 + 16].rearrange("j t e -> t j e")
                ).then_inc(s.h("EXC"), 16)
    p.switch_sec("sync", rd)
    s.inc("EXC", 16 * 32)
    p.t_zread = s.val("EXC")


def emit_zln_wo(p):
    s = p.s
    nc = p.nc

    def w(eng):
        eng.wait_ge(s.h("EXC"), p.t_zread)
    t_ln = emit_ln_plain(p, lambda i: zown_ap(p, i), lambda i: zown_ap(p, i),
                         LN_EPS_A, wait=w)
    if ZLN_STAGE < 2:
        return
    t_tp = emit_transpose16(
        p, lambda i: zown_ap(p, i),
        lambda i: p.P1[:, 0:8, i * 128:(i + 1) * 128],
        lambda i: p.P2[:, 0:8, i * 128:(i + 1) * 128], t_ln)

    if ZLN_STAGE < 3:
        return
    gcol = p.SMF[:, 176:192]
    bcol = p.SCR[:, SCR_SP + 16:SCR_SP + 32]
    bworow = p.SCR[0:1, SCR_STG:SCR_STG + 2048]

    def g(eng):
        eng.wait_ge(s.h("TPS"), t_tp)
        p.ld_dma(eng, gcol, p.dt["ln_g"].rearrange("(c p) -> p c", p=128),
                 allow_slow_non_contiguous=True)
        p.ld_dma(eng, bcol, p.dt["ln_b"].rearrange("(c p) -> p c", p=128),
                 allow_slow_non_contiguous=True)
        for gc in range(16):
            p.ld_dma(eng, wores_ap(p, gc).rearrange("p (c f) -> p c f", f=512),
                     p.dt["WoT"][gc * 128:(gc + 1) * 128, :].rearrange(
                         "p (c f) -> p c f", f=512))
    p.sec("gpsimd", g)
    ld_wo = s.val("LD")

    with nc.psum_tensor("psR", [128, 2048], F32) as psR:
        def te1(eng):
            eng.wait_ge(s.h("LD"), ld_wo)
            for gc in range(16):
                for fs in range(4):
                    mm = eng.matmul(psR[0:1, fs * 512:(fs + 1) * 512],
                                    bcol[:, gc:gc + 1],
                                    wores_ap(p, gc)[:, fs * 512:(fs + 1) * 512],
                                    start=(gc == 0), stop=(gc == 15))
            mm.then_inc(s.inc("TE", 1), 1)
        p.sec("tensor", te1)
        t_brow = s.val("TE")

        def sc1(eng):
            eng.wait_ge(s.h("TE"), t_brow)
            eng.activation(bworow, psR[0:1, :], AF.Copy).then_inc(s.inc("SE", 1), 1)
        p.sec("scalar", sc1)
        t_brow_ev = s.val("SE")

    def v1(eng):
        eng.wait_ge(s.h("TE"), t_brow)
        for gc in range(16):
            op = eng.tensor_scalar(wores_ap(p, gc), wores_ap(p, gc),
                                   gcol[:, gc:gc + 1], None, ALU.mult)
        op.then_inc(s.inc("VE", 1), 1)
    p.sec("vector", v1)
    t_fold = s.val("VE")

    with (
        nc.psum_tensor("psWA", [128, 2048], F32) as psA,
        nc.psum_tensor("psWB", [128, 2048], F32) as psB,
    ):
        te0 = s.val("TE")
        ve0 = s.val("VE")
        te_ticks = {tt: te0 + tt + 1 for tt in range(16)}

        def te2(eng):
            eng.wait_ge(s.h("SE"), t_brow_ev)
            eng.wait_ge(s.h("VE"), t_fold)
            for tt in range(16):
                pp = psA if tt % 2 == 0 else psB
                if tt >= 2:
                    eng.wait_ge(s.h("VE"), ve0 + tt - 1)
                for gc in range(16):
                    lhs = lnzfm_ap(p, gc)[:, tt * 128:(tt + 1) * 128]
                    for fs in range(4):
                        eng.matmul(pp[:, fs * 512:(fs + 1) * 512], lhs,
                                   wores_ap(p, gc)[:, fs * 512:(fs + 1) * 512],
                                   start=(gc == 0), stop=False)
                for fs in range(4):
                    mm = eng.matmul(pp[:, fs * 512:(fs + 1) * 512],
                                    p.CONST[0:1, CI_ONES, :],
                                    bworow[:, fs * 512:(fs + 1) * 512],
                                    start=False, stop=True)
                mm.then_inc(s.inc("TE", 1), 1)
        p.sec("tensor", te2)

        def v2(eng):
            for tt in range(16):
                pp = psA if tt % 2 == 0 else psB
                eng.wait_ge(s.h("TE"), te_ticks[tt])
                eng.tensor_tensor(p.X[:, tt, :], pp[:], p.X[:, tt, :],
                                  ALU.add).then_inc(s.inc("VE", 1), 1)
        p.sec("vector", v2)
    p.t_x1 = s.val("VE")


def ph_store_x(p):
    s = p.s
    tv, tt_, ts_ = s.val("VE"), s.val("TE"), s.val("SE")

    def g(eng):
        eng.wait_ge(s.h("VE"), tv)
        eng.wait_ge(s.h("TE"), tt_)
        eng.wait_ge(s.h("SE"), ts_)
        eng.dma_start(out=p.y_out.rearrange("(s p) d -> p s d", p=128),
                      in_=p.X[:]).then_inc(s.inc("ST", 16), 16)
    p.sec("gpsimd", g)


def ph_final_wait(p):
    s = p.s

    def g(eng):
        eng.wait_ge(s.h("ST"), s.val("ST"))
    p.sec("gpsimd", g)



_PROG_CACHE = {}


def _get_prog():
    key = (DEBUG_PHASES, ATTN_A_STAGE, ZLN_STAGE, FFN_STAGE)
    if key not in _PROG_CACHE:
        _PROG_CACHE[key] = build_program()
    return _PROG_CACHE[key]


def _build_consts():
    c = np.zeros((16, 128, 128), dtype=np.float32)
    c[CI_IDENT] = np.eye(128, dtype=np.float32)
    c[CI_CAUSAL] = (np.arange(128)[None, :] >= np.arange(128)[:, None]).astype(np.float32)
    P = np.zeros((DH, DH), dtype=np.float32)
    for dp in range(64):
        P[dp + 64, dp] = -1.0
        P[dp, dp + 64] = 1.0
    c[CI_ROT] = P
    c[CI_ONES] = 1.0
    mb = np.arange(128) // MB
    c[CI_MUN:CI_MUN + 4] = -INV_N * (mb[None, :] > mb[:, None]).astype(np.float32)
    c[CI_MLN:CI_MLN + 4] = -INV_N * (mb[:, None] > mb[None, :]).astype(np.float32)
    c[CI_MONES] = -1.0
    c[CI_CADD] = np.where(np.arange(128)[None, :] > np.arange(128)[:, None],
                          -30000.0, 0.0)
    return c


def kernel(**inputs):
    nc = _get_prog()
    x = np.asarray(inputs["L0_attn_a_x"])

    WkT = np.ascontiguousarray(np.asarray(inputs["L0_attn_a_Wk"]).T)
    WvT = np.ascontiguousarray(np.asarray(inputs["L0_attn_a_Wv"]).T)
    WoT = np.ascontiguousarray(np.asarray(inputs["L0_attn_a_Wo"]).T)
    WQT = np.ascontiguousarray(np.asarray(inputs["L0_attn_b_W_Q"]).transpose(2, 0, 1).reshape(D, D))
    WKT = np.ascontiguousarray(np.asarray(inputs["L0_attn_b_W_K"]).transpose(2, 0, 1).reshape(D, D))
    WVT = np.ascontiguousarray(np.asarray(inputs["L0_attn_b_W_V"]).transpose(2, 0, 1).reshape(D, D))
    WOT = np.ascontiguousarray(np.asarray(inputs["L0_attn_b_W_O"]).transpose(0, 2, 1).reshape(D, D))

    consts = _build_consts()
    inv_freq = 1.0 / (10000.0 ** (np.arange(0, DH, 2, dtype=np.float32) / DH))

    in_maps = []
    for c in range(8):
        b, h2 = c // 2, c % 2
        t0 = TOK * h2
        pos = np.arange(t0, t0 + TOK, dtype=np.float32)
        freqs = pos[None, :] * np.concatenate([inv_freq, inv_freq])[:, None]
        m = {
            "x_own": x[b, t0:t0 + TOK, :],
            "WkT": WkT, "WvT": WvT, "WoT": WoT,
            "ln_g": np.asarray(inputs["L0_attn_a_ln_gamma"]),
            "ln_b": np.asarray(inputs["L0_attn_a_ln_beta"]),
            "WQT": WQT, "WKT": WKT, "WVT": WVT, "WOT": WOT,
            "agg_w": np.asarray(inputs["L0_ffn_agg_w"]),
            "agg_b": np.asarray(inputs["L0_ffn_agg_b"]),
            "pn_g": np.asarray(inputs["L0_ffn_pn_gamma"]),
            "pn_b": np.asarray(inputs["L0_ffn_pn_beta"]),
            "d1_w": np.asarray(inputs["L0_ffn_d1_w"])[:, c * HID_SLICE:(c + 1) * HID_SLICE],
            "d1_b": np.asarray(inputs["L0_ffn_d1_b"])[c * HID_SLICE:(c + 1) * HID_SLICE],
            "d2_w": np.asarray(inputs["L0_ffn_d2_w"])[c * HID_SLICE:(c + 1) * HID_SLICE, :],
            "d2_b": np.asarray(inputs["L0_ffn_d2_b"]),
            "bc_w": np.asarray(inputs["L0_ffn_bc_w"]),
            "bc_b": np.asarray(inputs["L0_ffn_bc_b"]),
            "gate_w": np.asarray(inputs["L0_ffn_gate_w"]),
            "gate_b": np.asarray(inputs["L0_ffn_gate_b"]),
            "out_g": np.asarray(inputs["L0_ffn_out_gamma"]),
            "out_b": np.asarray(inputs["L0_ffn_out_beta"]),
            "in_g": np.asarray(inputs["L0_ffn_in_gamma"]),
            "in_b": np.asarray(inputs["L0_ffn_in_beta"]),
            "rope_cos": np.cos(freqs),
            "rope_sin": np.sin(freqs),
            "consts": consts,
        }
        m = {k: np.ascontiguousarray(np.asarray(v), dtype=np.float32) for k, v in m.items()}
        in_maps.append(m)

    res = run_bass_kernel_spmd(nc, in_maps, core_ids=list(range(8)))
    out = np.empty((B, S, D), dtype=np.float32)
    for c in range(8):
        b, h2 = c // 2, c % 2
        out[b, TOK * h2:TOK * h2 + TOK, :] = res.results[c]["y_own"]
    return out


def _A(p, i, w=2048, n=4):
    return p.SCR[32 * i:32 * i + n, 2048:2048 + w]


def _B(p, i, w=2048, n=4):
    return p.SCR[32 * i:32 * i + n, 0:w]


def _t4buf(p, w=64):
    return p.SCR[:, 4160:4160 + w]


def emit_ln4(p, src, dst, eps, gamma_rep, beta_rep, psum4, smf_row=230):
    s = p.s
    st = p.SMF

    def v(eng):
        eng.wait_ge(s.h("VE"), s.val("VE"))
        for q in range(4):
            op = eng.bn_stats(st[0:4, smf_row + 6 * q:smf_row + 6 * q + 6],
                              src[:, q * 512:(q + 1) * 512])
        op.then_inc(s.inc("VE", 1), 1)
        eng.wait_ge(s.h("VE"), s.count["VE"])
        eng.bn_aggr(st[0:4, smf_row + 24:smf_row + 26],
                    st[0:4, smf_row:smf_row + 24]).then_inc(s.inc("VE", 1), 1)
        eng.wait_ge(s.h("VE"), s.count["VE"])
        eng.tensor_scalar(st[0:4, smf_row + 26:smf_row + 27],
                          st[0:4, smf_row + 25:smf_row + 26], eps, None,
                          ALU.add).then_inc(s.inc("VE", 1), 1)
    p.sec("vector", v)
    t1 = s.val("VE")

    def sc(eng):
        eng.wait_ge(s.h("VE"), t1)
        eng.activation(p.SMF[0:4, smf_row + 27:smf_row + 28],
                       p.SMF[0:4, smf_row + 26:smf_row + 27],
                       AF.Sqrt).then_inc(s.inc("SE", 1), 1)
    p.sec("scalar", sc)
    t2 = s.val("SE")

    def v2(eng):
        eng.wait_ge(s.h("SE"), t2)
        eng.reciprocal(st[0:4, smf_row + 28:smf_row + 29],
                       st[0:4, smf_row + 27:smf_row + 28]).then_inc(s.inc("VE", 1), 1)
        eng.wait_ge(s.h("VE"), s.count["VE"])
        if gamma_rep is None:
            eng.tensor_scalar(dst, src, st[0:4, smf_row + 24:smf_row + 25],
                              st[0:4, smf_row + 28:smf_row + 29],
                              ALU.subtract, ALU.mult).then_inc(s.inc("VE", 1), 1)
        else:
            eng.tensor_scalar(psum4[0:4, :], src,
                              st[0:4, smf_row + 24:smf_row + 25],
                              st[0:4, smf_row + 28:smf_row + 29],
                              ALU.subtract, ALU.mult).then_inc(s.inc("VE", 1), 1)
            eng.wait_ge(s.h("VE"), s.count["VE"])
            eng.tensor_tensor(psum4[0:4, :], psum4[0:4, :], gamma_rep,
                              ALU.mult).then_inc(s.inc("VE", 1), 1)
            eng.wait_ge(s.h("VE"), s.count["VE"])
            eng.tensor_tensor(psum4[0:4, :], psum4[0:4, :], beta_rep,
                              ALU.add).then_inc(s.inc("VE", 1), 1)
            eng.wait_ge(s.h("VE"), s.count["VE"])
            eng.tensor_copy(dst, psum4[0:4, :]).then_inc(s.inc("VE", 1), 1)
    p.sec("vector", v2)
    return s.val("VE")


def emit_t4(p, src4, dstT, psum, nchunk=16):
    s = p.s
    ve_in = s.val("VE")
    se_in = s.val("SE")

    def te(eng):
        eng.wait_ge(s.h("VE"), ve_in)
        eng.wait_ge(s.h("SE"), se_in)
        for c in range(nchunk):
            mm = eng.matmul(psum[:, c * 4:(c + 1) * 4], src4[:, c * 128:(c + 1) * 128],
                            p.CONST[0:4, CI_IDENT, 0:4], start=True, stop=True)
        mm.then_inc(s.inc("TE", 1), 1)
    p.sec("tensor", te)
    t = s.val("TE")

    def sc(eng):
        eng.wait_ge(s.h("TE"), t)
        eng.activation(dstT, psum[:, 0:4 * nchunk], AF.Copy).then_inc(s.inc("SE", 1), 1)
    p.sec("scalar", sc)
    return s.val("SE")


def emit_small_proj(p, lhsT_ap, w_dram, w_cols, out4, psum, bias_rep4, nchunk=16):
    s = p.s
    nb = w_cols // 128
    wv = p.SCR[:, 0:2048].rearrange("p (c f) -> p c f", f=128)
    se_pre = s.val("SE")
    ve_pre = s.val("VE")
    for b_ in range(nb):
        te_pre = s.val("TE")

        def g(eng, b_=b_, te_pre=te_pre):
            if te_pre > 0:
                eng.wait_ge(s.h("TE"), te_pre)
            p.ld_dma(eng, wv[:],
                     w_dram[:, b_ * 128:(b_ + 1) * 128].rearrange(
                         "(c p) f -> p c f", p=128)[:, 0:nchunk, :])
        p.sec("gpsimd", g)
        ld1 = s.val("LD")

        def te(eng, b_=b_, ld1=ld1):
            eng.wait_ge(s.h("LD"), ld1)
            if b_ == 0:
                eng.wait_ge(s.h("SE"), se_pre)
                eng.wait_ge(s.h("VE"), ve_pre)
            for gc in range(nchunk):
                mm = eng.matmul(psum[0:4, b_ * 128:(b_ + 1) * 128], lhsT_ap(gc),
                                wv[:, gc, :], start=(gc == 0), stop=(gc == nchunk - 1))
            mm.then_inc(s.inc("TE", 1), 1)
        p.sec("tensor", te)
    t = s.val("TE")

    def v(eng):
        eng.wait_ge(s.h("TE"), t)
        eng.tensor_tensor(psum[0:4, 0:w_cols], psum[0:4, 0:w_cols], bias_rep4,
                          ALU.add).then_inc(s.inc("VE", 1), 1)
    p.sec("vector", v)
    t2 = s.val("VE")

    def sc(eng):
        eng.wait_ge(s.h("VE"), t2)
        eng.activation(out4, psum[0:4, 0:w_cols], AF.Copy).then_inc(s.inc("SE", 1), 1)
    p.sec("scalar", sc)
    return s.val("SE")


def d1_ap(p, gc):
    return p.P2[:, gc // 2, (gc % 2) * 1024:(gc % 2) * 1024 + 1024]


def d2_ap(p, gc):
    return p.P2[:, 8 + gc, :]


def ph_ffn(p, idx):
    s = p.s
    nc = p.nc

    ve_pre = s.val("VE")

    def w(eng):
        eng.wait_ge(s.h("VE"), ve_pre)
        eng.wait_ge(s.h("TE"), s.val("TE"))
        eng.wait_ge(s.h("SE"), s.val("SE"))
    t_ln = emit_ln_plain(p, lambda i: p.X[:, i, :], lambda i: p.P2[:, i, :],
                         LN_EPS_OUTER, wait=w)
    t_lnT = emit_transpose16(
        p, lambda i: p.P2[:, i, :],
        lambda i: p.P1[:, 0:8, i * 128:(i + 1) * 128],
        lambda i: p.P1[:, 8:16, i * 128:(i + 1) * 128], t_ln)

    def g1(eng):
        eng.wait_ge(s.h("VE"), t_ln)
        eng.wait_ge(s.h("LD"), s.val("LD"))
        eng.dma_start(out=p.cc["row_i"], in_=p.P2[127:128, 15, :]
                      ).then_inc(s.inc("LD", 16), 16)
        eng.wait_ge(s.h("LD"), s.val("LD"))
        eng.collective_compute("AllGather", ALU.bypass, replica_groups=p.RG8,
                               ins=[p.cc["row_i"]], outs=[p.cc["row_o"]]
                               ).then_inc(s.inc("CC", 1), 1)
    p.sec("gpsimd", g1)
    cc_ag = s.val("CC")

    def g2(eng):
        eng.wait_ge(s.h("CC"), cc_ag)
        p.ld_dma(eng, _A(p, 0), p.cc["row_o"][1:8:2, :])
        for r in range(4):
            p.ld_dma(eng, p.SCR[32 + r:33 + r, 2048:4096],
                     p.dt["in_g"].rearrange("(a d) -> a d", a=1))
            p.ld_dma(eng, p.SCR[64 + r:65 + r, 2048:4096],
                     p.dt["in_b"].rearrange("(a d) -> a d", a=1))
        p.ld_dma(eng, p.SMF[:, 208:224],
                 p.dt["gate_b"].rearrange("(c p) -> p c", p=128),
                 allow_slow_non_contiguous=True)
        eng.wait_ge(s.h("TPS"), t_lnT)
        for gc in range(16):
            p.ld_dma(eng, d1_ap(p, gc), p.dt["d1_w"][gc * 128:(gc + 1) * 128, :])
        for gc in range(8):
            p.ld_dma(eng, d2_ap(p, gc), p.dt["d2_w"][gc * 128:(gc + 1) * 128, :])
    p.sec("gpsimd", g2)
    ld_const = s.val("LD")

    def vw(eng):
        eng.wait_ge(s.h("LD"), ld_const)
    p.sec("vector", vw)

    with nc.psum_tensor("pX%d" % idx, [4, 2048], F32) as pX:
        t_xn = emit_ln4(p, _A(p, 0), _A(p, 0), LN_EPS_F, _A(p, 1), _A(p, 2), pX)

    def gagb(eng):
        eng.wait_ge(s.h("VE"), t_xn)
        for r in range(4):
            p.ld_dma(eng, p.SCR[32 + r:33 + r, 2048:4096],
                     p.dt["agg_b"].rearrange("(a d) -> a d", a=1))
    p.sec("gpsimd", gagb)
    ld_agb = s.val("LD")

    def vagb(eng):
        eng.wait_ge(s.h("LD"), ld_agb)
    p.sec("vector", vagb)

    with (
        nc.psum_tensor("pT%d" % idx, [128, 64], F32) as pT,
        nc.psum_tensor("pS%d" % idx, [4, 2048], F32) as pS,
    ):
        t = emit_t4(p, _A(p, 0), _t4buf(p), pT)

        def wv(eng):
            eng.wait_ge(s.h("SE"), t)
        p.sec("vector", wv)
        xn4T = lambda gc: p.SCR[:, 4160 + gc * 4:4160 + (gc + 1) * 4]
        t_ve = emit_small_proj(p, xn4T, p.dt["agg_w"], 2048, _A(p, 3), pS,
                               _A(p, 1))

        def gsw(eng):
            eng.wait_ge(s.h("SE"), t_ve)
            for r in range(4):
                p.ld_dma(eng, p.SCR[32 + r:33 + r, 2048:4096],
                         p.dt["pn_g"].rearrange("(a d) -> a d", a=1))
                p.ld_dma(eng, p.SCR[64 + r:65 + r, 2048:4096],
                         p.dt["pn_b"].rearrange("(a d) -> a d", a=1))
                p.ld_dma(eng, p.SCR[r:r + 1, 0:1024],
                         p.dt["d1_b"].rearrange("(a d) -> a d", a=1))
                p.ld_dma(eng, p.SCR[32 + r:33 + r, 0:2048],
                         p.dt["d2_b"].rearrange("(a d) -> a d", a=1))
        p.sec("gpsimd", gsw)
        ld_sw = s.val("LD")

        def vsw(eng):
            eng.wait_ge(s.h("LD"), ld_sw)
        p.sec("vector", vsw)

        if FFN_STAGE < 2:
            return
        for step in range(4):
            def wv2(eng, t=t_ve):
                eng.wait_ge(s.h("SE"), t)
            p.sec("vector", wv2)
            emit_ln4(p, _A(p, 3), _A(p, 0), LN_EPS_F, _A(p, 1), _A(p, 2), pS)
            t = emit_t4(p, _A(p, 0), _t4buf(p), pT)
            tlnT = lambda gc: p.SCR[:, 4160 + gc * 4:4160 + (gc + 1) * 4]
            def teh(eng, t=t):
                eng.wait_ge(s.h("SE"), t)
                for hb in range(2):
                    for gc in range(16):
                        mm = eng.matmul(pS[0:4, hb * 512:(hb + 1) * 512], tlnT(gc),
                                        d1_ap(p, gc)[:, hb * 512:(hb + 1) * 512],
                                        start=(gc == 0), stop=(gc == 15))
                mm.then_inc(s.inc("TE", 1), 1)
            p.sec("tensor", teh)
            t_h = s.val("TE")

            def sch(eng):
                eng.wait_ge(s.h("TE"), t_h)
                eng.activation(_A(p, 0, w=1024), pS[0:4, 0:1024],
                               AF.Copy).then_inc(s.inc("SE", 1), 1)
            p.sec("scalar", sch)
            t_g0 = s.val("SE")

            def vh1(eng):
                eng.wait_ge(s.h("SE"), t_g0)
                eng.tensor_tensor(_A(p, 0, w=1024), _A(p, 0, w=1024),
                                  _B(p, 0, w=1024), ALU.add
                                  ).then_inc(s.inc("VE", 1), 1)
            p.sec("vector", vh1)
            t_vh = s.val("VE")

            def sch2(eng):
                eng.wait_ge(s.h("VE"), t_vh)
                eng.activation(_A(p, 0, w=1024), _A(p, 0, w=1024),
                               AF.Gelu).then_inc(s.inc("SE", 1), 1)
            p.sec("scalar", sch2)
            t_g = s.val("SE")

            def wv3(eng):
                eng.wait_ge(s.h("SE"), t_g)
            p.sec("vector", wv3)
            t = emit_t4(p, _A(p, 0, w=1024), _t4buf(p, 32), pT, nchunk=8)
            hT = lambda gc: p.SCR[:, 4160 + gc * 4:4160 + (gc + 1) * 4]

            def ted(eng, t=t):
                eng.wait_ge(s.h("SE"), t)
                for fb in range(4):
                    for gc in range(8):
                        mm = eng.matmul(pS[0:4, fb * 512:(fb + 1) * 512], hT(gc),
                                        d2_ap(p, gc)[:, fb * 512:(fb + 1) * 512],
                                        start=(gc == 0), stop=(gc == 7))
                mm.then_inc(s.inc("TE", 1), 1)
            p.sec("tensor", ted)
            t_d = s.val("TE")

            def scd(eng):
                eng.wait_ge(s.h("TE"), t_d)
                eng.activation(_B(p, 3), pS[0:4, :], AF.Copy
                               ).then_inc(s.inc("SE", 1), 1)
            p.sec("scalar", scd)
            t_p = s.val("SE")

            def gar(eng):
                eng.wait_ge(s.h("SE"), t_p)
                eng.wait_ge(s.h("LD"), s.val("LD"))
                eng.dma_start(out=p.cc["th_i"], in_=_B(p, 3)
                              ).then_inc(s.inc("LD", 16), 16)
                eng.wait_ge(s.h("LD"), s.val("LD"))
                eng.collective_compute("AllReduce", ALU.add, replica_groups=p.RG8,
                                       ins=[p.cc["th_i"]], outs=[p.cc["th_o"]]
                                       ).then_inc(s.inc("CC", 1), 1)
                eng.wait_ge(s.h("CC"), s.val("CC"))
                p.ld_dma(eng, _B(p, 2), p.cc["th_o"])
            p.sec("gpsimd", gar)
            ld_ar = s.val("LD")

            def vup(eng):
                eng.wait_ge(s.h("LD"), ld_ar)
                eng.tensor_copy(pS[0:4, :], _B(p, 2)).then_inc(s.inc("VE", 1), 1)
                eng.wait_ge(s.h("VE"), s.count["VE"])
                eng.tensor_tensor(pS[0:4, :], pS[0:4, :], _B(p, 1),
                                  ALU.add).then_inc(s.inc("VE", 1), 1)
                eng.wait_ge(s.h("VE"), s.count["VE"])
                eng.tensor_tensor(_A(p, 3), _A(p, 3), pS[0:4, :],
                                  ALU.add).then_inc(s.inc("VE", 1), 1)
            p.sec("vector", vup)
            t = 0
            se_now = s.val("SE")
            ve_now = s.val("VE")

            def wsync(eng):
                eng.wait_ge(s.h("VE"), ve_now)
            p.sec("scalar", wsync)
            t = se_now

        if FFN_STAGE < 3:
            return
        def wv4(eng):
            eng.wait_ge(s.h("VE"), s.val("VE"))
            eng.tensor_copy(_A(p, 0), _A(p, 3)).then_inc(s.inc("VE", 1), 1)
        p.sec("vector", wv4)

        def gbcb(eng):
            eng.wait_ge(s.h("VE"), s.val("VE"))
            for r in range(4):
                p.ld_dma(eng, p.SCR[32 + r:33 + r, 2048:4096],
                         p.dt["bc_b"].rearrange("(a d) -> a d", a=1))
        p.sec("gpsimd", gbcb)
        ld_bcb = s.val("LD")

        def vbcb(eng):
            eng.wait_ge(s.h("LD"), ld_bcb)
        p.sec("vector", vbcb)

        t = emit_t4(p, _A(p, 0), _t4buf(p), pT)

        def wv5(eng):
            eng.wait_ge(s.h("SE"), t)
        p.sec("vector", wv5)
        thT = lambda gc: p.SCR[:, 4160 + gc * 4:4160 + (gc + 1) * 4]
        t_bc = emit_small_proj(p, thT, p.dt["bc_w"], 2048, _A(p, 2), pS,
                               _A(p, 1))

        def sel2(eng):
            for arm in eng.switch_core_id(n=128):
                b = (arm.logical % 8) // 2
                eng.wait_ge(s.h("SE"), t_bc)
                eng.wait_ge(s.h("EXC"), s.val("EXC"))
                eng.dma_start(out=p.SCR[0:1, 0:2048],
                              in_=p.SCR[64 + b:65 + b, 2048:4096]).then_inc(s.h("EXC"), 16)
        p.sec("sync", sel2)
        s.inc("EXC", 16)
        t_sel = s.val("EXC")

        def tebc(eng):
            eng.wait_ge(s.h("EXC"), t_sel)
            for c in range(16):
                mm = eng.matmul(pT[:, c:c + 1],
                                p.SCR[0:1, c * 128:(c + 1) * 128],
                                p.CONST[0:1, CI_ONES, 0:1], start=True, stop=True)
            mm.then_inc(s.inc("TE", 1), 1)
        p.sec("tensor", tebc)
        t_bcc = s.val("TE")

        def scbc(eng):
            eng.wait_ge(s.h("TE"), t_bcc)
            eng.activation(p.SMF[:, 192:208], pT[:, 0:16], AF.Copy
                           ).then_inc(s.inc("SE", 1), 1)
        p.sec("scalar", scbc)
        t_bccol = s.val("SE")

    if FFN_STAGE < 4:
        return
    with (
        nc.psum_tensor("pGA%d" % idx, [128, 2048], F32) as pGA,
        nc.psum_tensor("pGB%d" % idx, [128, 2048], F32) as pGB,
    ):
        ps = [pGA, pGB]
        sig = p.SCR[:, 2048:4096]
        te0 = s.val("TE")
        se0 = s.val("SE")
        ve0 = s.val("VE")
        mm_done = {fc: te0 + fc + 1 for fc in range(16)}
        sig_done = {fc: se0 + fc + 1 for fc in range(16)}
        cmb_done = {fc: ve0 + fc + 1 for fc in range(16)}

        def gw(eng):
            for fc in range(16):
                if fc >= 1:
                    eng.wait_ge(s.h("TE"), mm_done[fc - 1])
                else:
                    eng.wait_ge(s.h("SE"), t_bccol)
                p.ld_dma(eng, p.SCR[:, 0:2048].rearrange("p (c f) -> p c f", f=128),
                         p.dt["gate_w"][:, fc * 128:(fc + 1) * 128].rearrange(
                             "(c p) f -> p c f", p=128))
        p.sec("gpsimd", gw)
        gld = s.val("LD")

        def gte(eng):
            for fc in range(16):
                eng.wait_ge(s.h("LD"), gld - 16 * (15 - fc))
                if fc == 0:
                    eng.wait_ge(s.h("SE"), t_bccol)
                if fc >= 2:
                    eng.wait_ge(s.h("SE"), sig_done[fc - 2])
                for tq in range(4):
                    for gc in range(16):
                        mm = eng.matmul(ps[fc % 2][:, tq * 512:(tq + 1) * 512],
                                        p.SCR[:, gc * 128:(gc + 1) * 128],
                                        p.P1[:, gc, tq * 512:(tq + 1) * 512],
                                        start=(gc == 0), stop=(gc == 15))
                mm.then_inc(s.inc("TE", 1), 1)
        p.sec("tensor", gte)

        def gsc(eng):
            for fc in range(16):
                eng.wait_ge(s.h("TE"), mm_done[fc])
                if fc >= 1:
                    eng.wait_ge(s.h("VE"), cmb_done[fc - 1])
                eng.activation(sig, ps[fc % 2][:], AF.Sigmoid,
                               bias=p.SMF[:, 208 + fc:209 + fc]
                               ).then_inc(s.inc("SE", 1), 1)
        p.sec("scalar", gsc)

        def gve(eng):
            for fc in range(16):
                eng.wait_ge(s.h("SE"), sig_done[fc])
                eng.scalar_tensor_tensor(p.P2[:, fc, :], sig,
                                         p.SMF[:, 192 + fc:193 + fc],
                                         p.P1[:, fc, :], ALU.mult,
                                         ALU.add).then_inc(s.inc("VE", 1), 1)
        p.sec("vector", gve)
    t_out = s.val("VE")

    if FFN_STAGE < 5:
        return
    def tpo(eng):
        eng.wait_ge(s.h("VE"), t_out)
        for fc in range(16):
            eng.dma_start(out=p.P1[:, 0:16, fc * 128:(fc + 1) * 128],
                          in_=p.P2[:, fc, :], transpose=True
                          ).then_inc(s.inc("TPS", 16), 16)
    p.sec("sync", tpo)
    t_tp2 = s.val("TPS")

    def g3(eng):
        eng.wait_ge(s.h("TPS"), t_tp2)
        p.ld_dma(eng, p.SCR[0:1, 0:2048], p.dt["out_g"].rearrange("(a d) -> a d", a=1))
    p.sec("gpsimd", g3)
    ld_ob = s.val("LD")

    with nc.psum_tensor("pR2%d" % idx, [128, 2048], F32) as pR2:
        def ter(eng):
            eng.wait_ge(s.h("LD"), ld_ob)
            for fs in range(4):
                mm = eng.matmul(pR2[:, fs * 512:(fs + 1) * 512],
                                p.CONST[0:1, CI_ONES, :],
                                p.SCR[0:1, fs * 512:(fs + 1) * 512],
                                start=True, stop=True)
            mm.then_inc(s.inc("TE", 1), 1)
        p.sec("tensor", ter)
        t_g1 = s.val("TE")

        def scr1(eng):
            eng.wait_ge(s.h("TE"), t_g1)
            eng.activation(p.SCR[:, 0:2048], pR2[:], AF.Copy).then_inc(s.inc("SE", 1), 1)
        p.sec("scalar", scr1)
        t_gr = s.val("SE")

        def g4(eng):
            eng.wait_ge(s.h("TE"), t_g1)
            p.ld_dma(eng, p.SCR[0:1, 2048:4096], p.dt["out_b"].rearrange("(a d) -> a d", a=1))
        p.sec("gpsimd", g4)
        ld_ob2 = s.val("LD")

        def ter2(eng):
            eng.wait_ge(s.h("SE"), t_gr)
            eng.wait_ge(s.h("LD"), ld_ob2)
            for fs in range(4):
                mm = eng.matmul(pR2[:, fs * 512:(fs + 1) * 512],
                                p.CONST[0:1, CI_ONES, :],
                                p.SCR[0:1, 2048 + fs * 512:2048 + (fs + 1) * 512],
                                start=True, stop=True)
            mm.then_inc(s.inc("TE", 1), 1)
        p.sec("tensor", ter2)
        t_g2 = s.val("TE")

        def scr2(eng):
            eng.wait_ge(s.h("TE"), t_g2)
            eng.activation(p.SCR[:, 2048:4096], pR2[:], AF.Copy).then_inc(s.inc("SE", 1), 1)
        p.sec("scalar", scr2)
        t_br = s.val("SE")

    def w2(eng):
        eng.wait_ge(s.h("TPS"), t_tp2)
        eng.wait_ge(s.h("SE"), t_br)
    t_ln2 = emit_ln_plain(p, lambda i: p.P1[:, i, :], lambda i: p.P1[:, i, :],
                          LN_EPS_F, wait=w2)

    grep_ = p.SCR[:, 0:2048]
    brep_ = p.SCR[:, 2048:4096]

    def vf(eng):
        eng.wait_ge(s.h("VE"), t_ln2)
        for i in range(NSTRIP):
            eng.tensor_tensor(p.P1[:, i, :], p.P1[:, i, :], grep_,
                              ALU.mult).then_inc(s.inc("VE", 1), 1)
            eng.wait_ge(s.h("VE"), s.count["VE"])
            eng.tensor_tensor(p.P1[:, i, :], p.P1[:, i, :], brep_,
                              ALU.add).then_inc(s.inc("VE", 1), 1)
            eng.wait_ge(s.h("VE"), s.count["VE"])
            eng.tensor_tensor(p.X[:, i, :], p.X[:, i, :], p.P1[:, i, :],
                              ALU.add).then_inc(s.inc("VE", 1), 1)
    p.sec("vector", vf)
    p.t_ffn = s.val("VE")




SM_SCALE = 1.0 / (DH ** 0.5)


def ph_attn_b(p):
    s = p.s
    nc = p.nc

    ve_pre = s.val("VE")

    def w(eng):
        eng.wait_ge(s.h("VE"), ve_pre)
        eng.wait_ge(s.h("TE"), s.val("TE"))
        eng.wait_ge(s.h("SE"), s.val("SE"))
    t_ln = emit_ln_plain(p, lambda i: p.X[:, i, :], lambda i: p.P2[:, i, :],
                         LN_EPS_OUTER, wait=w)
    p.t_lnT = emit_transpose16(
        p, lambda i: p.P2[:, i, :],
        lambda i: p.P1[:, 0:8, i * 128:(i + 1) * 128],
        lambda i: p.P1[:, 8:16, i * 128:(i + 1) * 128], t_ln)

    lnfm = lambda gc: p.P1[:, gc, :]
    with (
        nc.psum_tensor("psQA", [128, 2048], F32) as psA,
        nc.psum_tensor("psQB", [128, 2048], F32) as psB,
    ):
        emit_proj_to_shared(p, "WQT", lnfm, 0, p.sh["qkv"], False, psA, psB)
        emit_proj_to_shared(p, "WKT", lnfm, 1, p.sh["qkv"], False, psA, psB)
        emit_proj_to_shared(p, "WVT", lnfm, 2, p.sh["qkv"], False, psA, psB)

    bar = emit_barrier(p)

    kT_buf = lambda par: p.P1[:, 2 * par:2 * par + 2, :].rearrange("p a b -> p (a b)")
    qT_buf = lambda par: p.P1[:, 4 + 2 * par:6 + 2 * par, :].rearrange("p a b -> p (a b)")
    vh_buf = lambda par: p.P1[:, 8 + 3 * par:11 + 3 * par, :].rearrange(
        "p a b -> p (a b)")[:, 0:4608].rearrange("p (a b) -> p a b", b=144)
    zslot = lambda h: p.P2[:, 2 * h:2 * h + 2, :].rearrange("p a b -> p (a b)")

    def vone(eng):
        eng.wait_ge(s.h("CC"), bar)
        eng.memset(p.SMF[:, 300:301], -6.0)
        eng.memset(p.P1[:, 8:14, :], 1.0).then_inc(s.inc("VE", 1), 1)
    p.sec("vector", vone)
    t_ones = s.val("VE")

    with (
        nc.psum_tensor("sA_b", [128, 1024], F32) as sA,
        nc.psum_tensor("sB_b", [128, 1024], F32) as sB,
        nc.psum_tensor("zAcc", [128, 2048], F32) as zAcc,
    ):
        sc_ps = [sA, sB]
        te_head = {}
        for h in range(8):
            par = h % 2
            te_now = s.val("TE")
            exc_after = s.val("EXC") + 16 * 6

            def rd(eng, parity, h=h, par=par, te_now=te_now, exc_after=exc_after):
                hg = 8 * parity + h
                eng.wait_ge(s.h("CC"), bar)
                eng.wait_ge(s.h("VE"), t_ones)
                if h >= 2:
                    eng.wait_ge(s.h("TE"), te_head[h - 2])
                elif te_now > 0:
                    eng.wait_ge(s.h("TE"), te_now)
                for wr in range(2):
                    eng.dma_start(out=p.P1[:, 2 * par + wr, :],
                                  in_=p.sh["qkv"][wr, 1, hg, :, :]
                                  ).then_inc(s.h("EXC"), 16)
                    eng.dma_start(out=p.P1[:, 4 + 2 * par + wr, :],
                                  in_=p.sh["qkv"][wr, 0, hg, :, :]
                                  ).then_inc(s.h("EXC"), 16)
                    eng.dma_start(out=vh_buf(par)[:, 16 * wr:16 * wr + 16, 0:128],
                                  in_=p.sh["qkv"][wr, 2, hg, :, :], transpose=True
                                  ).then_inc(s.h("EXC"), 16)
                eng.wait_ge(s.h("EXC"), exc_after)
            p.switch_sec("sync", rd)
            s.inc("EXC", 16 * 6)
            t_rd = s.val("EXC")
            emit_attnb_head(p, h, par, t_rd, kT_buf(par), qT_buf(par),
                            vh_buf(par), zslot(h), sc_ps, zAcc)
            te_head[h] = s.val("TE")

    emit_zb_collect_out(p)


def emit_attnb_head(p, h, par, t_rd, kT, qT, vh, zsl, sc_ps, zAcc):
    s = p.s
    RC = 308
    CELL = 4224

    for half in range(2):
        for b in range(4):
            qcs = [half * 16 + b * 4 + j for j in range(4)]
            ap = sc_ps[b % 2]
            ve_pre = s.val("VE")

            def tea(eng, qcs=qcs, ap=ap, ve_pre=ve_pre, half=half, b=b):
                if half == 0 and b == 0:
                    eng.wait_ge(s.h("EXC"), t_rd)
                eng.wait_ge(s.h("VE"), ve_pre)
                for j, qc in enumerate(qcs):
                    k0 = max(0, qc * 128 - 128)
                    c0 = j * 256 + (128 if qc == 0 else 0)
                    mm = eng.matmul(ap[:, c0:j * 256 + 256],
                                    qT[:, qc * 128:(qc + 1) * 128],
                                    kT[:, k0:qc * 128 + 128],
                                    start=(j % 2 == 0), stop=True,
                                    skip_group_check=True)
                mm.then_inc(s.inc("TE", 1), 1)
            p.sec("tensor", tea)
            t_mm = s.val("TE")

            def vea(eng, qcs=qcs, ap=ap, b=b, t_mm=t_mm):
                eng.wait_ge(s.h("TE"), t_mm)
                for j, qc in enumerate(qcs):
                    eng.tensor_tensor(ap[:, j * 256 + 128:j * 256 + 256],
                                      ap[:, j * 256 + 128:j * 256 + 256],
                                      p.CONST[:, CI_CADD, :],
                                      ALU.add).then_inc(s.inc("VE", 1), 1)
                eng.wait_ge(s.h("VE"), s.count["VE"])
                for j, qc in enumerate(qcs):
                    c0 = j * 256 + (128 if qc == 0 else 0)
                    cc = CELL + (b % 2) * 4 + j
                    eng.reduce_max(p.SCR[:, cc:cc + 1], ap[:, c0:j * 256 + 256],
                                   axis=mybir.AxisListType.X
                                   ).then_inc(s.inc("VE", 1), 1)
            p.sec("vector", vea)
            t_red = s.val("VE")

            def tet(eng, qcs=qcs, b=b, half=half, t_red=t_red):
                eng.wait_ge(s.h("VE"), t_red)
                for j, qc in enumerate(qcs):
                    seg = qc - half * 16
                    cc = CELL + (b % 2) * 4 + j
                    mm = eng.matmul(zAcc[0:1, seg * 128:(seg + 1) * 128],
                                    p.SCR[:, cc:cc + 1],
                                    p.CONST[:, CI_IDENT, :],
                                    start=(j == 0), stop=True,
                                    skip_group_check=True)
                mm.then_inc(s.inc("TE", 1), 1)
            p.sec("tensor", tet)

        t_tp = s.val("TE")

        def ver(eng, half=half, t_tp=t_tp):
            eng.wait_ge(s.h("TE"), t_tp)
            eng.tensor_scalar(zsl[0:1, half * 2048:(half + 1) * 2048],
                              zAcc[0:1, 0:2048], 0.0, None,
                              ALU.add).then_inc(s.inc("VE", 1), 1)
        p.sec("vector", ver)

    t_bc = s.val("VE")

    PT = [p.SCR[:, 0:2048], p.SCR[:, 2048:4096]]
    last_exp = {0: None, 1: None}
    last_pv = {0: None, 1: None}
    for sp in range(4):
        span0 = sp * 1024
        nmc = sp * 8 + 8
        for mc in range(nmc):
            vstart = max(0, mc * 128 - span0)
            pp = sc_ps[mc % 2]
            le = last_exp[mc % 2]
            lp = last_pv[mc % 2]

            def tes(eng, mc=mc, span0=span0, vstart=vstart, pp=pp, sp=sp,
                    t_bc=t_bc, le=le):
                if sp == 0 and mc < 2:
                    eng.wait_ge(s.h("VE"), t_bc)
                if le is not None:
                    eng.wait_ge(s.h("SE"), le)
                for half in range(2):
                    if (half + 1) * 512 <= vstart:
                        continue
                    eng.matmul(pp[:, half * 512:(half + 1) * 512],
                               kT[:, mc * 128:(mc + 1) * 128],
                               qT[:, span0 + half * 512:span0 + (half + 1) * 512],
                               start=True, stop=False)
                    eng.matmul(pp[:, half * 512:(half + 1) * 512],
                               p.CONST[0:1, CI_MONES, 0:128],
                               zsl[0:1, span0 + half * 512:span0 + (half + 1) * 512],
                               start=False, stop=True).then_inc(s.inc("TE", 1), 1)
            p.sec("tensor", tes)
            t_mm = s.val("TE")

            def scs(eng, mc=mc, vstart=vstart, pp=pp, t_mm=t_mm, lp=lp):
                eng.wait_ge(s.h("TE"), t_mm)
                if lp is not None:
                    eng.wait_ge(s.h("TE"), lp)
                eng.activation(PT[mc % 2][:, vstart:1024], pp[:, vstart:1024],
                               AF.Exp, bias=p.SMF[:, 300:301],
                               scale=SM_SCALE).then_inc(s.inc("SE", 1), 1)
            p.sec("scalar", scs)
            last_exp[mc % 2] = s.val("SE")
            t_exp = s.val("SE")

            dq = mc * 128 - span0
            t_pt = s.val("VE")
            if 0 <= dq < 1024:
                def vm(eng, mc=mc, dq=dq, t_exp=t_exp):
                    eng.wait_ge(s.h("SE"), t_exp)
                    eng.tensor_tensor(PT[mc % 2][:, dq:dq + 128],
                                      PT[mc % 2][:, dq:dq + 128],
                                      p.CONST[:, CI_CAUSAL, :],
                                      ALU.mult).then_inc(s.inc("VE", 1), 1)
                p.sec("vector", vm)
                t_pt = s.val("VE")

            def tev(eng, mc=mc, sp=sp, vstart=vstart, t_exp=t_exp, t_pt=t_pt):
                eng.wait_ge(s.h("SE"), t_exp)
                eng.wait_ge(s.h("VE"), t_pt)
                for j in range(vstart // 128, 8):
                    mm = eng.matmul(zAcc[:, j * 256:j * 256 + 132],
                                    PT[mc % 2][:, j * 128:(j + 1) * 128],
                                    vh[:, mc, 0:132],
                                    start=(mc == 0 and j % 2 == 0),
                                    stop=(mc == 8 * sp + j),
                                    skip_group_check=True)
                mm.then_inc(s.inc("TE", 1), 1)
            p.sec("tensor", tev)
            last_pv[mc % 2] = s.val("TE")

        t_pv = s.val("TE")

        def vz(eng, sp=sp, t_pv=t_pv):
            eng.wait_ge(s.h("TE"), t_pv)
            for j in range(8):
                eng.wait_ge(s.h("VE"), s.count["VE"])
                eng.reciprocal(p.SMF[:, RC:RC + 1],
                               zAcc[:, j * 256 + 128:j * 256 + 129]
                               ).then_inc(s.inc("VE", 1), 1)
                eng.wait_ge(s.h("VE"), s.count["VE"])
                eng.tensor_scalar(zsl[:, (sp * 8 + j) * 128:(sp * 8 + j + 1) * 128],
                                  zAcc[:, j * 256:j * 256 + 128],
                                  p.SMF[:, RC:RC + 1], None,
                                  ALU.mult).then_inc(s.inc("VE", 1), 1)
        p.sec("vector", vz)
        t_ev = s.val("VE")

        def tw(eng, t_ev=t_ev):
            eng.wait_ge(s.h("VE"), t_ev)
        p.sec("tensor", tw)


def emit_zb_collect_out(p):
    s = p.s
    nc = p.nc
    ve_done = s.val("VE")

    def wr(eng, parity):
        eng.wait_ge(s.h("VE"), ve_done)
        for h in range(8):
            hg = 8 * parity + h
            eng.dma_start(out=p.sh["zb"][hg, 0:16].rearrange("j t e -> t j e"),
                          in_=p.P2[:, 2 * h, :].rearrange("p (j e) -> p j e", e=128)
                          ).then_inc(s.h("EXC"), 16)
            eng.dma_start(out=p.sh["zb"][hg, 16:32].rearrange("j t e -> t j e"),
                          in_=p.P2[:, 2 * h + 1, :].rearrange("p (j e) -> p j e", e=128)
                          ).then_inc(s.h("EXC"), 16)
    p.switch_sec("sync", wr)
    s.inc("EXC", 16 * 16)

    bar = emit_barrier(p)

    def rd(eng, parity):
        eng.wait_ge(s.h("CC"), bar)
        J0 = 16 * parity
        for hg in range(16):
            eng.dma_start(
                out=p.P1[:, 0:8, hg * 128:(hg + 1) * 128],
                in_=p.sh["zb"][hg, J0:J0 + 8].rearrange("j t e -> t j e")
                ).then_inc(s.h("EXC"), 16)
            eng.dma_start(
                out=p.P1[:, 8:16, hg * 128:(hg + 1) * 128],
                in_=p.sh["zb"][hg, J0 + 8:J0 + 16].rearrange("j t e -> t j e")
                ).then_inc(s.h("EXC"), 16)
    p.switch_sec("sync", rd)
    s.inc("EXC", 16 * 32)
    t_zread = s.val("EXC")

    def tp(eng):
        eng.wait_ge(s.h("EXC"), t_zread)
        for i in range(NSTRIP):
            eng.dma_start(out=p.P2[:, 0:8, i * 128:(i + 1) * 128],
                          in_=p.P1[:, i, 0:1024],
                          transpose=True).then_inc(s.inc("TPS", 16), 16)
            eng.dma_start(out=p.P2[:, 8:16, i * 128:(i + 1) * 128],
                          in_=p.P1[:, i, 1024:2048], transpose=True
                          ).then_inc(s.inc("TPS", 16), 16)
    p.sec("sync", tp)
    t_tp = s.val("TPS")

    def g(eng):
        eng.wait_ge(s.h("TPS"), t_tp)
        for gc in range(16):
            p.ld_dma(eng, p.P1[:, gc, :].rearrange("p (c f) -> p c f", f=512),
                     p.dt["WOT"][gc * 128:(gc + 1) * 128, :].rearrange(
                         "p (c f) -> p c f", f=512))
    p.sec("gpsimd", g)
    ld_wo = s.val("LD")

    with (
        nc.psum_tensor("psOA", [128, 2048], F32) as psA,
        nc.psum_tensor("psOB", [128, 2048], F32) as psB,
    ):
        te0 = s.val("TE")
        ve0 = s.val("VE")
        te_ticks = {tt: te0 + tt + 1 for tt in range(16)}

        def te2(eng):
            eng.wait_ge(s.h("LD"), ld_wo)
            for tt in range(16):
                pp = psA if tt % 2 == 0 else psB
                if tt >= 2:
                    eng.wait_ge(s.h("VE"), ve0 + tt - 1)
                for gc in range(16):
                    lhs = p.P2[:, gc, tt * 128:(tt + 1) * 128]
                    for fs in range(4):
                        mm = eng.matmul(pp[:, fs * 512:(fs + 1) * 512], lhs,
                                        p.P1[:, gc, fs * 512:(fs + 1) * 512],
                                        start=(gc == 0), stop=(gc == 15))
                mm.then_inc(s.inc("TE", 1), 1)
        p.sec("tensor", te2)

        def v2(eng):
            for tt in range(16):
                pp = psA if tt % 2 == 0 else psB
                eng.wait_ge(s.h("TE"), te_ticks[tt])
                eng.tensor_tensor(p.X[:, tt, :], pp[:], p.X[:, tt, :],
                                  ALU.add).then_inc(s.inc("VE", 1), 1)
        p.sec("vector", v2)
    p.t_x3 = s.val("VE")


# version 04 (final)
# speedup vs baseline: 1.0237x; 1.0237x over previous
import numpy as np
import ml_dtypes
import concourse.bass as bass
import concourse.mybir as mybir
from concourse.bass_utils import run_bass_kernel_spmd

F32 = mybir.dt.float32
BF16 = mybir.dt.float16
AF = mybir.ActivationFunctionType
ALU = mybir.AluOpType

B, S, D, H, DH, MB = 4, 4096, 2048, 16, 128, 16
TOK = 2048
NSTRIP = TOK // 128
NC = D // 128
NBLK = S // 128
INV_N = 2.0 / (MB * DH)
LN_EPS_OUTER = 1e-5
LN_EPS_A = 1e-5
LN_EPS_F = 1e-6
HID_SLICE = 8192 // 8

DEBUG_PHASES = None
FFN_STAGE = 9
ZLN_STAGE = 9
ATTN_A_STAGE = 9


class SemBank:
    def __init__(self):
        self.handles = {}
        self.count = {}

    def add(self, name, h):
        self.handles[name] = h
        self.count[name] = 0

    def inc(self, name, n):
        self.count[name] += n
        return self.handles[name]

    def val(self, name):
        return self.count[name]

    def h(self, name):
        return self.handles[name]


class Prog:
    def __init__(self, nc, block, sems):
        self.nc = nc
        self.block = block
        self.s = sems

    def sec(self, engname, fn):
        getattr(self.block, engname)(fn)

    def ld_dma(self, eng, out, in_, **kw):
        eng.wait_ge(self.s.h("LD"), self.s.val("LD"))
        return eng.dma_start(out=out, in_=in_, **kw).then_inc(self.s.inc("LD", 16), 16)

    def switch_sec(self, engname, body_fn):
        def fn(eng):
            for arm in eng.switch_core_id(n=128):
                body_fn(eng, arm.logical % 2)
        self.sec(engname, fn)


CI_IDENT, CI_CAUSAL, CI_ROT, CI_ONES = 0, 1, 2, 3
CI_MUN = 4
CI_MLN = 8
CI_MONES = 12
CI_CADD = 13

SCR_W0 = 0
SCR_STG = 2048
SCR_SP = 4096
SCN_W = 0
SCN_NT = 1024
SCN_NN = 1536
SCN_N2T = 2048
SCN_N2 = 2560
SCN_N4T = 3072
SCN_B = 3584
SCN_G = 4096
SCN_KH = 4608


def build_program():
    nc = bass.Bass("TRN2", target_bir_lowering=False)

    dt = {}
    def din(name, shape, dtype=F32):
        dt[name] = nc.dram_tensor(name, list(shape), dtype, kind="ExternalInput").ap()
        return dt[name]

    din("x_own", [TOK, D])
    y_out = nc.dram_tensor("y_own", [TOK, D], F32, kind="ExternalOutput").ap()

    din("WkT", [D, D]); din("WvT", [D, D]); din("WoT", [D, D])
    din("ln_g", [D]); din("ln_b", [D])
    din("WQT", [D, D]); din("WKT", [D, D]); din("WVT", [D, D]); din("WOT", [D, D])
    din("agg_w", [D, D]); din("agg_b", [D])
    din("pn_g", [D]); din("pn_b", [D])
    din("d1_w", [D, HID_SLICE]); din("d1_b", [HID_SLICE])
    din("d2_w", [HID_SLICE, D]); din("d2_b", [D])
    din("bc_w", [D, D]); din("bc_b", [D])
    din("gate_w", [D, D]); din("gate_b", [D])
    din("out_g", [D]); din("out_b", [D])
    din("in_g", [D]); din("in_b", [D])
    din("rope_cos", [DH, TOK], F32)
    din("rope_sin", [DH, TOK], F32)
    din("consts", [16, 128, 128], F32)

    sh_kv = nc.dram_tensor("sh_kv", [2, 2, H, DH, TOK], BF16, kind="Internal",
                           addr_space="Shared").ap()
    sh_za = nc.dram_tensor("sh_za", [H, NBLK, 128, DH], BF16, kind="Internal",
                           addr_space="Shared").ap()
    sh_qkv = nc.dram_tensor("sh_qkv", [2, 3, H, DH, TOK], BF16, kind="Internal",
                            addr_space="Shared").ap()
    sh_zb = nc.dram_tensor("sh_zb", [H, NBLK, 128, DH], BF16, kind="Internal",
                           addr_space="Shared").ap()
    cc_bar_i = nc.dram_tensor("cc_bar_i", [1, 8], F32, kind="Internal").ap()
    cc_bar_o = nc.dram_tensor("cc_bar_o", [1, 8], F32, kind="Internal").ap()
    cc_row_i = nc.dram_tensor("cc_row_i", [1, D], F32, kind="Internal").ap()
    cc_row_o = nc.dram_tensor("cc_row_o", [8, D], F32, kind="Internal").ap()
    cc_th_i = nc.dram_tensor("cc_th_i", [B, D], F32, kind="Internal").ap()
    cc_th_o = nc.dram_tensor("cc_th_o", [B, D], F32, kind="Internal").ap()

    sems = SemBank()
    with (
        nc.sbuf_tensor("X", [128, NSTRIP, D], BF16) as X,
        nc.sbuf_tensor("P1", [128, 16, 2048], BF16) as P1,
        nc.sbuf_tensor("P2", [128, 16, 2048], BF16) as P2,
        nc.sbuf_tensor("SCR", [128, 5120], BF16) as SCR,
        nc.sbuf_tensor("CONST", [128, 16, 128], BF16) as CONST,
        nc.sbuf_tensor("SMF", [128, 448], F32) as SMF,
        nc.semaphore("LD") as _ld,
        nc.semaphore("TPS") as _tps,
        nc.semaphore("EXC") as _exc,
        nc.semaphore("ST") as _st,
        nc.semaphore("CC") as _cc,
        nc.semaphore("TE") as _te,
        nc.semaphore("VE") as _ve,
        nc.semaphore("SE") as _se,
        nc.semaphore("GE") as _ge,
        nc.Block() as block,
    ):
        for nm, hh in [("LD", _ld), ("TPS", _tps), ("EXC", _exc), ("ST", _st),
                       ("CC", _cc), ("TE", _te), ("VE", _ve), ("SE", _se), ("GE", _ge)]:
            sems.add(nm, hh)

        p = Prog(nc, block, sems)
        p.dt = dt
        p.X, p.P1, p.P2, p.SCR, p.CONST, p.SMF = X, P1, P2, SCR, CONST, SMF
        p.y_out = y_out
        p.sh = dict(kv=sh_kv, za=sh_za, qkv=sh_qkv, zb=sh_zb)
        p.cc = dict(bar_i=cc_bar_i, bar_o=cc_bar_o, row_i=cc_row_i, row_o=cc_row_o,
                    th_i=cc_th_i, th_o=cc_th_o)
        p.RG8 = [list(range(8))]

        phases = [
            ph_load_input,
            ph_ln_outer,
            ph_attn_a,
            lambda p: ph_ffn(p, 1),
            ph_attn_b,
            lambda p: ph_ffn(p, 2),
        ]
        n = len(phases) if DEBUG_PHASES is None else DEBUG_PHASES
        for ph in phases[:n]:
            ph(p)
        ph_store_x(p)
        ph_final_wait(p)

    return nc



def ph_load_input(p):
    s = p.s
    dt = p.dt

    def g(eng):
        p.ld_dma(eng, p.X[:], dt["x_own"].rearrange("(s p) d -> p s d", p=128))
        p.ld_dma(eng, p.CONST[:], dt["consts"].rearrange("c p f -> p c f"))
        eng.memset(p.SMF[:, 440:448], 0.0).then_inc(s.inc("GE", 1), 1)
        eng.wait_ge(s.h("GE"), s.count["GE"])
        p.ld_dma(eng, p.cc["bar_i"], p.SMF[0:1, 440:448])
    p.sec("gpsimd", g)
    p.ld_x = s.val("LD")


def emit_ln_plain(p, src_ap_fn, dst_ap_fn, eps, wait=None):
    s = p.s
    stats = p.SMF

    def v(eng):
        if wait:
            wait(eng)
        for i in range(NSTRIP):
            src = src_ap_fn(i)
            if i:
                eng.wait_ge(s.h("VE"), s.count["VE"])
            for q in range(4):
                op = eng.bn_stats(stats[:, 32 + 6 * q:32 + 6 * q + 6],
                                  src[:, q * 512:(q + 1) * 512])
            op.then_inc(s.inc("VE", 1), 1)
            eng.wait_ge(s.h("VE"), s.count["VE"])
            eng.bn_aggr(stats[:, 2 * i:2 * i + 2],
                        stats[:, 32:56]).then_inc(s.inc("VE", 1), 1)
        eng.wait_ge(s.h("VE"), s.count["VE"])
        eng.tensor_scalar(stats[:, 128:144], stats[:, 1:32:2], eps, None,
                          ALU.add).then_inc(s.inc("VE", 1), 1)
    p.sec("vector", v)
    t_var = s.val("VE")

    def sc(eng):
        eng.wait_ge(s.h("VE"), t_var)
        eng.activation(stats[:, 144:160], stats[:, 128:144],
                       AF.Sqrt).then_inc(s.inc("SE", 1), 1)
    p.sec("scalar", sc)
    t_sqrt = s.val("SE")

    def v2(eng):
        eng.wait_ge(s.h("SE"), t_sqrt)
        eng.reciprocal(stats[:, 160:176],
                       stats[:, 144:160]).then_inc(s.inc("VE", 1), 1)
        eng.wait_ge(s.h("VE"), s.count["VE"])
        for i in range(NSTRIP):
            eng.tensor_scalar(dst_ap_fn(i), src_ap_fn(i),
                              stats[:, 2 * i:2 * i + 1],
                              stats[:, 160 + i:160 + i + 1],
                              ALU.subtract, ALU.mult).then_inc(s.inc("VE", 1), 1)
    p.sec("vector", v2)
    return s.val("VE")


def emit_transpose16(p, src_ap_fn, dst_lo, dst_hi, wait_ve):
    s = p.s

    def sy(eng):
        eng.wait_ge(s.h("VE"), wait_ve)
        for i in range(NSTRIP):
            src = src_ap_fn(i)
            eng.dma_start(out=dst_lo(i), in_=src[:, 0:1024], transpose=True
                          ).then_inc(s.inc("TPS", 16), 16)
            eng.dma_start(out=dst_hi(i), in_=src[:, 1024:2048], transpose=True
                          ).then_inc(s.inc("TPS", 16), 16)
    p.sec("sync", sy)
    return s.val("TPS")


def ph_ln_outer(p):
    s = p.s

    def w(eng):
        eng.wait_ge(s.h("LD"), p.ld_x)
    t = emit_ln_plain(p, lambda i: p.X[:, i, :], lambda i: p.P2[:, i, :],
                      LN_EPS_OUTER, wait=w)
    p.t_lnT = emit_transpose16(
        p, lambda i: p.P2[:, i, :],
        lambda i: p.P1[:, 0:8, i * 128:(i + 1) * 128],
        lambda i: p.P1[:, 8:16, i * 128:(i + 1) * 128], t)


def emit_barrier(p):
    s = p.s
    exc_t, tps_t = s.val("EXC"), s.val("TPS")

    def g(eng):
        eng.wait_ge(s.h("EXC"), exc_t)
        eng.wait_ge(s.h("TPS"), tps_t)
        eng.wait_ge(s.h("LD"), 48)
        eng.collective_compute("AllReduce", ALU.add, replica_groups=p.RG8,
                               ins=[p.cc["bar_i"]], outs=[p.cc["bar_o"]]
                               ).then_inc(s.inc("CC", 1), 1)
    p.sec("gpsimd", g)
    return s.val("CC")


def emit_proj_to_shared(p, wname, lnfm, kvslot, shared, rope, psA, psB,
                        extra_wait_ld=None):
    s = p.s
    wsl = [p.SCR[:, SCR_W0:SCR_W0 + 2048].rearrange("p (c f) -> p c f", f=128),
           p.P2[:, 12, :].rearrange("p (c f) -> p c f", f=128)]
    stg = p.SCR[:, SCR_STG:SCR_STG + 2048]
    rot_tmp = p.P2[:, 10, :]
    ps = [psA, psB]

    ld0 = s.val("LD")
    te0 = s.val("TE")
    se0 = s.val("SE")
    ve0 = s.val("VE")
    exc0 = s.val("EXC")
    per_fc_te = 2 if rope else 1
    mm_done = {fc: te0 + fc * per_fc_te + 1 for fc in range(NC)}
    rot_done = {fc: te0 + fc * per_fc_te + 2 for fc in range(NC)}
    evac_done = {fc: se0 + fc + 1 for fc in range(NC)}
    rope_done = {fc: ve0 + (fc + 1) * 3 for fc in range(NC)}
    route_done = {fc: exc0 + (fc + 1) * 16 for fc in range(NC)}

    def g2(eng):
        for fc in range(NC):
            if fc >= 2:
                eng.wait_ge(s.h("TE"), mm_done[fc - 2])
            else:
                eng.wait_ge(s.h("TPS"), p.t_lnT)
                if te0 > 0:
                    eng.wait_ge(s.h("TE"), te0)
                if ve0 > 0:
                    eng.wait_ge(s.h("VE"), ve0)
                if se0 > 0:
                    eng.wait_ge(s.h("SE"), se0)
            p.ld_dma(eng, wsl[fc % 2][:],
                     p.dt[wname][:, fc * 128:(fc + 1) * 128].rearrange(
                         "(c p) f -> p c f", p=128))
    p.sec("gpsimd", g2)

    def te(eng):
        for fc in range(NC):
            eng.wait_ge(s.h("LD"), ld0 + (fc + 1) * 16)
            if fc == 0:
                eng.wait_ge(s.h("TPS"), p.t_lnT)
                eng.wait_ge(s.h("VE"), ve0)
                eng.wait_ge(s.h("SE"), se0)
                eng.wait_ge(s.h("EXC"), exc0)
                if extra_wait_ld is not None:
                    eng.wait_ge(s.h("LD"), extra_wait_ld)
            if fc >= 2:
                eng.wait_ge(s.h("SE"), evac_done[fc - 2])
                if rope:
                    eng.wait_ge(s.h("VE"), rope_done[fc - 2])
            w = wsl[fc % 2]
            pp = ps[fc % 2]
            for tq in range(4):
                for gc in range(NC):
                    mm = eng.matmul(pp[:, tq * 512:(tq + 1) * 512], w[:, gc, :],
                                    lnfm(gc)[:, tq * 512:(tq + 1) * 512],
                                    start=(gc == 0), stop=(gc == NC - 1))
            mm.then_inc(s.inc("TE", 1), 1)
            if rope:
                eng.wait_ge(s.h("SE"), evac_done[fc])
                for tq in range(4):
                    mm = eng.matmul(pp[:, tq * 512:(tq + 1) * 512],
                                    p.CONST[:, CI_ROT, :],
                                    stg[:, tq * 512:(tq + 1) * 512],
                                    start=True, stop=True)
                mm.then_inc(s.inc("TE", 1), 1)
    p.sec("tensor", te)

    def sc(eng):
        for fc in range(NC):
            eng.wait_ge(s.h("TE"), mm_done[fc])
            if rope and fc >= 1:
                eng.wait_ge(s.h("VE"), rope_done[fc - 1])
            if not rope and fc >= 1:
                eng.wait_ge(s.h("EXC"), route_done[fc - 1])
            eng.activation(stg[:], ps[fc % 2][:], AF.Copy).then_inc(s.inc("SE", 1), 1)
    p.sec("scalar", sc)

    if rope:
        outbox = [p.P2[:, 11, :], p.P2[:, 13, :]]

        def ve(eng):
            for fc in range(NC):
                eng.wait_ge(s.h("TE"), rot_done[fc])
                if fc >= 2:
                    eng.wait_ge(s.h("EXC"), route_done[fc - 2])
                eng.tensor_tensor(rot_tmp[:], ps[fc % 2][:], p.P2[:, 9, :], ALU.mult)
                eng.tensor_tensor(stg[:], stg[:], p.P2[:, 8, :],
                                  ALU.mult).then_inc(s.inc("VE", 2), 2)
                eng.wait_ge(s.h("VE"), s.count["VE"])
                eng.tensor_tensor(outbox[fc % 2][:], stg[:], rot_tmp[:],
                                  ALU.add).then_inc(s.inc("VE", 1), 1)
        p.sec("vector", ve)
        src_of = lambda fc: outbox[fc % 2]
        wait_tick = lambda fc: ("VE", rope_done[fc])
    else:
        src_of = lambda fc: stg
        wait_tick = lambda fc: ("SE", evac_done[fc])

    def route(eng, parity):
        ex0 = exc0
        for fc in range(NC):
            nm, tk = wait_tick(fc)
            eng.wait_ge(s.h(nm), tk)
            eng.wait_ge(s.h("EXC"), ex0 + 16 * fc)
            eng.dma_start(out=shared[parity, kvslot, fc, :, :], in_=src_of(fc)
                          ).then_inc(s.h("EXC"), 16)
    p.switch_sec("sync", route)
    s.inc("EXC", 16 * NC)


def kT_ap(p, h, Jg):
    ar = p.P1 if Jg < 16 else p.P2
    j = Jg % 16
    return ar[:, h, j * 128:(j + 1) * 128]


def vh_ap(p, h, Jg):
    ar = p.P1 if Jg < 16 else p.P2
    j = Jg % 16
    return ar[:, 8 + h, j * 128:(j + 1) * 128]


def zown_ap(p, i):
    return (p.P1 if i < 8 else p.P2)[:, 8 + (i % 8), :]


def lnzfm_ap(p, gc):
    return (p.P1 if gc < 8 else p.P2)[:, gc % 8, :]


def wores_ap(p, gc):
    return (p.P1 if gc < 8 else p.P2)[:, 8 + (gc % 8), :]


def ph_attn_a(p):
    s = p.s
    nc = p.nc

    def g(eng):
        eng.wait_ge(s.h("TPS"), p.t_lnT)
        p.ld_dma(eng, p.P2[:, 8, :], p.dt["rope_cos"])
        p.ld_dma(eng, p.P2[:, 9, :], p.dt["rope_sin"])
    p.sec("gpsimd", g)
    ld_rope = s.val("LD")

    lnfm = lambda gc: p.P1[:, gc, :]

    with (
        nc.psum_tensor("psA", [128, 2048], F32) as psA,
        nc.psum_tensor("psB", [128, 2048], F32) as psB,
    ):
        emit_proj_to_shared(p, "WkT", lnfm, 0, p.sh["kv"], True, psA, psB,
                            extra_wait_ld=ld_rope)
        if ATTN_A_STAGE < 2:
            return
        emit_proj_to_shared(p, "WvT", lnfm, 1, p.sh["kv"], False, psA, psB)

    if ATTN_A_STAGE < 3:
        return
    bar1 = emit_barrier(p)
    if ATTN_A_STAGE < 4:
        return

    def reads(eng, parity):
        eng.wait_ge(s.h("CC"), bar1)
        for h in range(8):
            hg = 8 * parity + h
            eng.dma_start(out=p.P1[:, h, :], in_=p.sh["kv"][0, 0, hg, :, :]
                          ).then_inc(s.h("EXC"), 16)
            eng.dma_start(out=p.P2[:, h, :], in_=p.sh["kv"][1, 0, hg, :, :]
                          ).then_inc(s.h("EXC"), 16)
            eng.dma_start(out=p.P1[:, 8 + h, :].rearrange("p (c f) -> p c f", f=128),
                          in_=p.sh["kv"][0, 1, hg, :, :], transpose=True
                          ).then_inc(s.h("EXC"), 16)
            eng.dma_start(out=p.P2[:, 8 + h, :].rearrange("p (c f) -> p c f", f=128),
                          in_=p.sh["kv"][1, 1, hg, :, :], transpose=True
                          ).then_inc(s.h("EXC"), 16)
    p.switch_sec("sync", reads)
    s.inc("EXC", 16 * 32)
    t_reads = s.val("EXC")
    if ATTN_A_STAGE < 5:
        return

    def v0(eng):
        eng.wait_ge(s.h("EXC"), t_reads)
        for h in range(8):
            op = eng.tensor_copy(p.SCR[:, SCN_W + h * 128:SCN_W + (h + 1) * 128],
                                 p.CONST[:, CI_IDENT, :])
        op.then_inc(s.inc("VE", 1), 1)
    p.sec("vector", v0)
    t_prep = s.val("VE")
    if ATTN_A_STAGE < 6:
        return

    emit_scan(p, t_prep)
    if ATTN_A_STAGE < 7:
        return
    emit_z_collect(p)
    if ATTN_A_STAGE < 8:
        return
    emit_zln_wo(p)


def emit_scan(p, t_prep):
    s = p.s
    nc = p.nc
    p._scn_v_done = None

    with (
        nc.psum_tensor("sA", [128, 512], F32) as sA,
        nc.psum_tensor("sBase", [128, 512], F32) as sBase,
        nc.psum_tensor("sAp1", [128, 512], F32) as sAp1,
        nc.psum_tensor("sAp2", [128, 512], F32) as sAp2,
        nc.psum_tensor("sDW", [128, 512], F32) as sDW,
        nc.psum_tensor("sKH", [128, 512], F32) as sKH,
    ):
        first = True
        for Jg in range(NBLK):
            for grp in range(2):
                hs = [4 * grp + i for i in range(4)]
                emit_scan_block_group(p, Jg, grp, hs, sA, sBase, sAp1, sAp2,
                                      sDW, sKH, t_prep if first else None)
                first = False


def emit_scan_block_group(p, Jg, grp, hs, sA, sBase, sAp1, sAp2, sDW, sKH, t_prep):
    s = p.s
    SC = p.SCR
    c512 = lambda base: SC[:, base:base + 512]
    c128 = lambda base, i: SC[:, base + i * 128:base + (i + 1) * 128]
    prev_v = p._scn_v_done

    def te1(eng):
        if t_prep is not None:
            eng.wait_ge(s.h("VE"), t_prep)
        if prev_v is not None:
            eng.wait_ge(s.h("VE"), prev_v)
        for i, h in enumerate(hs):
            mm = eng.matmul(sA[:, i * 128:(i + 1) * 128], kT_ap(p, h, Jg),
                            kT_ap(p, h, Jg), start=True, stop=True)
        mm.then_inc(s.inc("TE", 1), 1)
        for i, h in enumerate(hs):
            mm = eng.matmul(sBase[:, i * 128:(i + 1) * 128], kT_ap(p, h, Jg),
                            SC[:, SCN_W + h * 128:SCN_W + (h + 1) * 128],
                            start=True, stop=True)
        mm.then_inc(s.inc("TE", 1), 1)
    p.sec("tensor", te1)
    tA = s.val("TE") - 1
    tBase = s.val("TE")

    def v1(eng):
        if t_prep is not None:
            eng.wait_ge(s.h("VE"), t_prep)
        eng.wait_ge(s.h("TE"), tA)
        eng.tensor_tensor(c512(SCN_NT), sA[:],
                          p.CONST[:, CI_MUN:CI_MUN + 4, :], ALU.mult)
        eng.tensor_tensor(c512(SCN_NN), sA[:],
                          p.CONST[:, CI_MLN:CI_MLN + 4, :], ALU.mult
                          ).then_inc(s.inc("VE", 2), 2)
        eng.wait_ge(s.h("TE"), tBase)
        eng.tensor_tensor(c512(SCN_B), sBase[:], _vh4(p, hs, Jg),
                          ALU.subtract).then_inc(s.inc("VE", 1), 1)
    p.sec("vector", v1)
    tMask = s.val("VE") - 1
    tB = s.val("VE")

    def te2(eng):
        eng.wait_ge(s.h("VE"), tMask)
        for i in range(4):
            mm = eng.matmul(sA[:, i * 128:(i + 1) * 128], c128(SCN_NN, i),
                            c128(SCN_NT, i), start=True, stop=True)
        mm.then_inc(s.inc("TE", 1), 1)
    p.sec("tensor", te2)
    tN2T = s.val("TE")

    def sc1(eng):
        eng.wait_ge(s.h("TE"), tN2T)
        eng.activation(c512(SCN_N2T), sA[:], AF.Copy).then_inc(s.inc("SE", 1), 1)
    p.sec("scalar", sc1)
    tN2Tev = s.val("SE")

    def te3(eng):
        eng.wait_ge(s.h("SE"), tN2Tev)
        for i in range(4):
            mm = eng.matmul(sA[:, i * 128:(i + 1) * 128], c128(SCN_NT, i),
                            c128(SCN_NN, i), start=True, stop=True)
        mm.then_inc(s.inc("TE", 1), 1)
    p.sec("tensor", te3)
    tN2 = s.val("TE")

    def sc2(eng):
        eng.wait_ge(s.h("TE"), tN2)
        eng.activation(c512(SCN_N2), sA[:], AF.Copy).then_inc(s.inc("SE", 1), 1)
    p.sec("scalar", sc2)
    tN2ev = s.val("SE")

    def te4(eng):
        eng.wait_ge(s.h("SE"), tN2ev)
        for i in range(4):
            mm = eng.matmul(sA[:, i * 128:(i + 1) * 128], c128(SCN_N2, i),
                            c128(SCN_N2T, i), start=True, stop=True)
        mm.then_inc(s.inc("TE", 1), 1)
    p.sec("tensor", te4)
    tN4 = s.val("TE")

    def sc3(eng):
        eng.wait_ge(s.h("TE"), tN4)
        eng.activation(c512(SCN_N4T), sA[:], AF.Copy).then_inc(s.inc("SE", 1), 1)
    p.sec("scalar", sc3)
    tN4ev = s.val("SE")

    def te5(eng):
        eng.wait_ge(s.h("VE"), tB)
        for i in range(4):
            eng.matmul(sAp1[:, i * 128:(i + 1) * 128], p.CONST[:, CI_IDENT, :],
                       c128(SCN_B, i), start=True, stop=False)
            mm = eng.matmul(sAp1[:, i * 128:(i + 1) * 128], c128(SCN_NT, i),
                            c128(SCN_B, i), start=False, stop=True)
        mm.then_inc(s.inc("TE", 1), 1)
    p.sec("tensor", te5)
    tY1 = s.val("TE")

    def sc4(eng):
        eng.wait_ge(s.h("TE"), tY1)
        eng.activation(c512(SCN_B), sAp1[:], AF.Copy).then_inc(s.inc("SE", 1), 1)
    p.sec("scalar", sc4)
    tY1ev = s.val("SE")

    def te6(eng):
        eng.wait_ge(s.h("SE"), tY1ev)
        for i in range(4):
            eng.matmul(sAp2[:, i * 128:(i + 1) * 128], p.CONST[:, CI_IDENT, :],
                       c128(SCN_B, i), start=True, stop=False)
            mm = eng.matmul(sAp2[:, i * 128:(i + 1) * 128], c128(SCN_N2T, i),
                            c128(SCN_B, i), start=False, stop=True)
        mm.then_inc(s.inc("TE", 1), 1)
    p.sec("tensor", te6)
    tY2 = s.val("TE")

    def sc5(eng):
        eng.wait_ge(s.h("TE"), tY2)
        eng.activation(c512(SCN_B), sAp2[:], AF.Copy).then_inc(s.inc("SE", 1), 1)
    p.sec("scalar", sc5)
    tY2ev = s.val("SE")

    def te7(eng):
        eng.wait_ge(s.h("SE"), tY2ev)
        eng.wait_ge(s.h("SE"), tN4ev)
        for i in range(4):
            eng.matmul(sAp1[:, i * 128:(i + 1) * 128], p.CONST[:, CI_IDENT, :],
                       c128(SCN_B, i), start=True, stop=False)
            mm = eng.matmul(sAp1[:, i * 128:(i + 1) * 128], c128(SCN_N4T, i),
                            c128(SCN_B, i), start=False, stop=True)
        mm.then_inc(s.inc("TE", 1), 1)
        for i, h in enumerate(hs):
            mm = eng.matmul(sKH[:, i * 128:(i + 1) * 128], kT_ap(p, h, Jg),
                            p.CONST[:, CI_IDENT, :], start=True, stop=True)
        mm.then_inc(s.inc("TE", 1), 1)
    p.sec("tensor", te7)
    tG = s.val("TE") - 1
    tKH = s.val("TE")

    def sc6(eng):
        eng.wait_ge(s.h("TE"), tG)
        eng.activation(c512(SCN_G), sAp1[:], AF.Copy).then_inc(s.inc("SE", 1), 1)
        eng.wait_ge(s.h("TE"), tKH)
        eng.activation(c512(SCN_KH), sKH[:], AF.Copy).then_inc(s.inc("SE", 1), 1)
    p.sec("scalar", sc6)
    tGev = s.val("SE") - 1
    tKHev = s.val("SE")

    def te8(eng):
        eng.wait_ge(s.h("SE"), tKHev)
        for i, h in enumerate(hs):
            mm = eng.matmul(sDW[:, i * 128:(i + 1) * 128], c128(SCN_KH, i),
                            c128(SCN_G, i), start=True, stop=True)
        mm.then_inc(s.inc("TE", 1), 1)
    p.sec("tensor", te8)
    tDW = s.val("TE")

    def v2(eng):
        eng.wait_ge(s.h("SE"), tGev)
        eng.wait_ge(s.h("TE"), tDW)
        for i, h in enumerate(hs):
            op = eng.tensor_tensor(kT_ap(p, h, Jg), c128(SCN_G, i), vh_ap(p, h, Jg),
                                   ALU.add)
        op.then_inc(s.inc("VE", 4), 4)
        eng.scalar_tensor_tensor(SC[:, SCN_W + grp * 512:SCN_W + (grp + 1) * 512],
                                 sDW[:], -INV_N,
                                 SC[:, SCN_W + grp * 512:SCN_W + (grp + 1) * 512],
                                 ALU.mult, ALU.add).then_inc(s.inc("VE", 1), 1)
    p.sec("vector", v2)
    p._scn_v_done = s.val("VE")


def _vh4(p, hs, Jg):
    ar = p.P1 if Jg < 16 else p.P2
    j = Jg % 16
    h0 = hs[0]
    return ar[:, 8 + h0:8 + h0 + 4, j * 128:(j + 1) * 128]


def emit_z_collect(p):
    s = p.s
    v_done = p._scn_v_done

    def wr(eng, parity):
        eng.wait_ge(s.h("VE"), v_done)
        for h in range(8):
            hg = 8 * parity + h
            eng.dma_start(out=p.sh["za"][hg, 0:16].rearrange("j t e -> t j e"),
                          in_=p.P1[:, h, :].rearrange("p (j e) -> p j e", e=128)
                          ).then_inc(s.h("EXC"), 16)
            eng.dma_start(out=p.sh["za"][hg, 16:32].rearrange("j t e -> t j e"),
                          in_=p.P2[:, h, :].rearrange("p (j e) -> p j e", e=128)
                          ).then_inc(s.h("EXC"), 16)
    p.switch_sec("sync", wr)
    s.inc("EXC", 16 * 16)

    bar = emit_barrier(p)

    def rd(eng, parity):
        eng.wait_ge(s.h("CC"), bar)
        J0 = 16 * parity
        for hg in range(16):
            eng.dma_start(
                out=p.P1[:, 8:16, hg * 128:(hg + 1) * 128],
                in_=p.sh["za"][hg, J0:J0 + 8].rearrange("j t e -> t j e")
                ).then_inc(s.h("EXC"), 16)
            eng.dma_start(
                out=p.P2[:, 8:16, hg * 128:(hg + 1) * 128],
                in_=p.sh["za"][hg, J0 + 8:J0 + 16].rearrange("j t e -> t j e")
                ).then_inc(s.h("EXC"), 16)
    p.switch_sec("sync", rd)
    s.inc("EXC", 16 * 32)
    p.t_zread = s.val("EXC")


def emit_zln_wo(p):
    s = p.s
    nc = p.nc

    def w(eng):
        eng.wait_ge(s.h("EXC"), p.t_zread)
    t_ln = emit_ln_plain(p, lambda i: zown_ap(p, i), lambda i: zown_ap(p, i),
                         LN_EPS_A, wait=w)
    if ZLN_STAGE < 2:
        return
    t_tp = emit_transpose16(
        p, lambda i: zown_ap(p, i),
        lambda i: p.P1[:, 0:8, i * 128:(i + 1) * 128],
        lambda i: p.P2[:, 0:8, i * 128:(i + 1) * 128], t_ln)

    if ZLN_STAGE < 3:
        return
    gcol = p.SMF[:, 176:192]
    bcol = p.SCR[:, SCR_SP + 16:SCR_SP + 32]
    bworow = p.SCR[0:1, SCR_STG:SCR_STG + 2048]

    def g(eng):
        eng.wait_ge(s.h("TPS"), t_tp)
        p.ld_dma(eng, gcol, p.dt["ln_g"].rearrange("(c p) -> p c", p=128),
                 allow_slow_non_contiguous=True)
        p.ld_dma(eng, bcol, p.dt["ln_b"].rearrange("(c p) -> p c", p=128),
                 allow_slow_non_contiguous=True)
        for gc in range(16):
            p.ld_dma(eng, wores_ap(p, gc).rearrange("p (c f) -> p c f", f=512),
                     p.dt["WoT"][gc * 128:(gc + 1) * 128, :].rearrange(
                         "p (c f) -> p c f", f=512))
    p.sec("gpsimd", g)
    ld_wo = s.val("LD")

    with nc.psum_tensor("psR", [128, 2048], F32) as psR:
        def te1(eng):
            eng.wait_ge(s.h("LD"), ld_wo)
            for gc in range(16):
                for fs in range(4):
                    mm = eng.matmul(psR[0:1, fs * 512:(fs + 1) * 512],
                                    bcol[:, gc:gc + 1],
                                    wores_ap(p, gc)[:, fs * 512:(fs + 1) * 512],
                                    start=(gc == 0), stop=(gc == 15))
            mm.then_inc(s.inc("TE", 1), 1)
        p.sec("tensor", te1)
        t_brow = s.val("TE")

        def sc1(eng):
            eng.wait_ge(s.h("TE"), t_brow)
            eng.activation(bworow, psR[0:1, :], AF.Copy).then_inc(s.inc("SE", 1), 1)
        p.sec("scalar", sc1)
        t_brow_ev = s.val("SE")

    def v1(eng):
        eng.wait_ge(s.h("TE"), t_brow)
        for gc in range(16):
            op = eng.tensor_scalar(wores_ap(p, gc), wores_ap(p, gc),
                                   gcol[:, gc:gc + 1], None, ALU.mult)
        op.then_inc(s.inc("VE", 1), 1)
    p.sec("vector", v1)
    t_fold = s.val("VE")

    with (
        nc.psum_tensor("psWA", [128, 2048], F32) as psA,
        nc.psum_tensor("psWB", [128, 2048], F32) as psB,
    ):
        te0 = s.val("TE")
        ve0 = s.val("VE")
        te_ticks = {tt: te0 + tt + 1 for tt in range(16)}

        def te2(eng):
            eng.wait_ge(s.h("SE"), t_brow_ev)
            eng.wait_ge(s.h("VE"), t_fold)
            for tt in range(16):
                pp = psA if tt % 2 == 0 else psB
                if tt >= 2:
                    eng.wait_ge(s.h("VE"), ve0 + tt - 1)
                for gc in range(16):
                    lhs = lnzfm_ap(p, gc)[:, tt * 128:(tt + 1) * 128]
                    for fs in range(4):
                        eng.matmul(pp[:, fs * 512:(fs + 1) * 512], lhs,
                                   wores_ap(p, gc)[:, fs * 512:(fs + 1) * 512],
                                   start=(gc == 0), stop=False)
                for fs in range(4):
                    mm = eng.matmul(pp[:, fs * 512:(fs + 1) * 512],
                                    p.CONST[0:1, CI_ONES, :],
                                    bworow[:, fs * 512:(fs + 1) * 512],
                                    start=False, stop=True)
                mm.then_inc(s.inc("TE", 1), 1)
        p.sec("tensor", te2)

        def v2(eng):
            for tt in range(16):
                pp = psA if tt % 2 == 0 else psB
                eng.wait_ge(s.h("TE"), te_ticks[tt])
                eng.tensor_tensor(p.X[:, tt, :], pp[:], p.X[:, tt, :],
                                  ALU.add).then_inc(s.inc("VE", 1), 1)
        p.sec("vector", v2)
    p.t_x1 = s.val("VE")


def ph_store_x(p):
    s = p.s
    tv, tt_, ts_ = s.val("VE"), s.val("TE"), s.val("SE")

    def g(eng):
        eng.wait_ge(s.h("VE"), tv)
        eng.wait_ge(s.h("TE"), tt_)
        eng.wait_ge(s.h("SE"), ts_)
        eng.dma_start(out=p.y_out.rearrange("(s p) d -> p s d", p=128),
                      in_=p.X[:]).then_inc(s.inc("ST", 16), 16)
    p.sec("gpsimd", g)


def ph_final_wait(p):
    s = p.s

    def g(eng):
        eng.wait_ge(s.h("ST"), s.val("ST"))
    p.sec("gpsimd", g)



_PROG_CACHE = {}


def _get_prog():
    key = (DEBUG_PHASES, ATTN_A_STAGE, ZLN_STAGE, FFN_STAGE)
    if key not in _PROG_CACHE:
        _PROG_CACHE[key] = build_program()
    return _PROG_CACHE[key]


def _build_consts():
    c = np.zeros((16, 128, 128), dtype=np.float32)
    c[CI_IDENT] = np.eye(128, dtype=np.float32)
    c[CI_CAUSAL] = (np.arange(128)[None, :] >= np.arange(128)[:, None]).astype(np.float32)
    P = np.zeros((DH, DH), dtype=np.float32)
    for dp in range(64):
        P[dp + 64, dp] = -1.0
        P[dp, dp + 64] = 1.0
    c[CI_ROT] = P
    c[CI_ONES] = 1.0
    mb = np.arange(128) // MB
    c[CI_MUN:CI_MUN + 4] = -INV_N * (mb[None, :] > mb[:, None]).astype(np.float32)
    c[CI_MLN:CI_MLN + 4] = -INV_N * (mb[:, None] > mb[None, :]).astype(np.float32)
    c[CI_MONES] = -1.0
    c[CI_CADD] = np.where(np.arange(128)[None, :] > np.arange(128)[:, None],
                          -30000.0, 0.0)
    return c


def kernel(**inputs):
    nc = _get_prog()
    x = np.asarray(inputs["L0_attn_a_x"])

    WkT = np.ascontiguousarray(np.asarray(inputs["L0_attn_a_Wk"]).T)
    WvT = np.ascontiguousarray(np.asarray(inputs["L0_attn_a_Wv"]).T)
    WoT = np.ascontiguousarray(np.asarray(inputs["L0_attn_a_Wo"]).T)
    WQT = np.ascontiguousarray(np.asarray(inputs["L0_attn_b_W_Q"]).transpose(2, 0, 1).reshape(D, D))
    WKT = np.ascontiguousarray(np.asarray(inputs["L0_attn_b_W_K"]).transpose(2, 0, 1).reshape(D, D))
    WVT = np.ascontiguousarray(np.asarray(inputs["L0_attn_b_W_V"]).transpose(2, 0, 1).reshape(D, D))
    WOT = np.ascontiguousarray(np.asarray(inputs["L0_attn_b_W_O"]).transpose(0, 2, 1).reshape(D, D))

    consts = _build_consts()
    inv_freq = 1.0 / (10000.0 ** (np.arange(0, DH, 2, dtype=np.float32) / DH))

    in_maps = []
    for c in range(8):
        b, h2 = c // 2, c % 2
        t0 = TOK * h2
        pos = np.arange(t0, t0 + TOK, dtype=np.float32)
        freqs = pos[None, :] * np.concatenate([inv_freq, inv_freq])[:, None]
        m = {
            "x_own": x[b, t0:t0 + TOK, :],
            "WkT": WkT, "WvT": WvT, "WoT": WoT,
            "ln_g": np.asarray(inputs["L0_attn_a_ln_gamma"]),
            "ln_b": np.asarray(inputs["L0_attn_a_ln_beta"]),
            "WQT": WQT, "WKT": WKT, "WVT": WVT, "WOT": WOT,
            "agg_w": np.asarray(inputs["L0_ffn_agg_w"]),
            "agg_b": np.asarray(inputs["L0_ffn_agg_b"]),
            "pn_g": np.asarray(inputs["L0_ffn_pn_gamma"]),
            "pn_b": np.asarray(inputs["L0_ffn_pn_beta"]),
            "d1_w": np.asarray(inputs["L0_ffn_d1_w"])[:, c * HID_SLICE:(c + 1) * HID_SLICE],
            "d1_b": np.asarray(inputs["L0_ffn_d1_b"])[c * HID_SLICE:(c + 1) * HID_SLICE],
            "d2_w": np.asarray(inputs["L0_ffn_d2_w"])[c * HID_SLICE:(c + 1) * HID_SLICE, :],
            "d2_b": np.asarray(inputs["L0_ffn_d2_b"]),
            "bc_w": np.asarray(inputs["L0_ffn_bc_w"]),
            "bc_b": np.asarray(inputs["L0_ffn_bc_b"]),
            "gate_w": np.asarray(inputs["L0_ffn_gate_w"]),
            "gate_b": np.asarray(inputs["L0_ffn_gate_b"]),
            "out_g": np.asarray(inputs["L0_ffn_out_gamma"]),
            "out_b": np.asarray(inputs["L0_ffn_out_beta"]),
            "in_g": np.asarray(inputs["L0_ffn_in_gamma"]),
            "in_b": np.asarray(inputs["L0_ffn_in_beta"]),
            "rope_cos": np.cos(freqs),
            "rope_sin": np.sin(freqs),
            "consts": consts,
        }
        m = {k: np.ascontiguousarray(np.asarray(v), dtype=np.float32) for k, v in m.items()}
        in_maps.append(m)

    res = run_bass_kernel_spmd(nc, in_maps, core_ids=list(range(8)))
    out = np.empty((B, S, D), dtype=np.float32)
    for c in range(8):
        b, h2 = c // 2, c % 2
        out[b, TOK * h2:TOK * h2 + TOK, :] = res.results[c]["y_own"]
    return out


def _A(p, i, w=2048, n=4):
    return p.SCR[32 * i:32 * i + n, 2048:2048 + w]


def _B(p, i, w=2048, n=4):
    return p.SCR[32 * i:32 * i + n, 0:w]


def _t4buf(p, w=64):
    return p.SCR[:, 4160:4160 + w]


def emit_ln4(p, src, dst, eps, gamma_rep, beta_rep, psum4, smf_row=230):
    s = p.s
    st = p.SMF

    def v(eng):
        eng.wait_ge(s.h("VE"), s.val("VE"))
        for q in range(4):
            op = eng.bn_stats(st[0:4, smf_row + 6 * q:smf_row + 6 * q + 6],
                              src[:, q * 512:(q + 1) * 512])
        op.then_inc(s.inc("VE", 1), 1)
        eng.wait_ge(s.h("VE"), s.count["VE"])
        eng.bn_aggr(st[0:4, smf_row + 24:smf_row + 26],
                    st[0:4, smf_row:smf_row + 24]).then_inc(s.inc("VE", 1), 1)
        eng.wait_ge(s.h("VE"), s.count["VE"])
        eng.tensor_scalar(st[0:4, smf_row + 26:smf_row + 27],
                          st[0:4, smf_row + 25:smf_row + 26], eps, None,
                          ALU.add).then_inc(s.inc("VE", 1), 1)
    p.sec("vector", v)
    t1 = s.val("VE")

    def sc(eng):
        eng.wait_ge(s.h("VE"), t1)
        eng.activation(p.SMF[0:4, smf_row + 27:smf_row + 28],
                       p.SMF[0:4, smf_row + 26:smf_row + 27],
                       AF.Sqrt).then_inc(s.inc("SE", 1), 1)
    p.sec("scalar", sc)
    t2 = s.val("SE")

    def v2(eng):
        eng.wait_ge(s.h("SE"), t2)
        eng.reciprocal(st[0:4, smf_row + 28:smf_row + 29],
                       st[0:4, smf_row + 27:smf_row + 28]).then_inc(s.inc("VE", 1), 1)
        eng.wait_ge(s.h("VE"), s.count["VE"])
        if gamma_rep is None:
            eng.tensor_scalar(dst, src, st[0:4, smf_row + 24:smf_row + 25],
                              st[0:4, smf_row + 28:smf_row + 29],
                              ALU.subtract, ALU.mult).then_inc(s.inc("VE", 1), 1)
        else:
            eng.tensor_scalar(psum4[0:4, :], src,
                              st[0:4, smf_row + 24:smf_row + 25],
                              st[0:4, smf_row + 28:smf_row + 29],
                              ALU.subtract, ALU.mult).then_inc(s.inc("VE", 1), 1)
            eng.wait_ge(s.h("VE"), s.count["VE"])
            eng.tensor_tensor(psum4[0:4, :], psum4[0:4, :], gamma_rep,
                              ALU.mult).then_inc(s.inc("VE", 1), 1)
            eng.wait_ge(s.h("VE"), s.count["VE"])
            eng.tensor_tensor(psum4[0:4, :], psum4[0:4, :], beta_rep,
                              ALU.add).then_inc(s.inc("VE", 1), 1)
            eng.wait_ge(s.h("VE"), s.count["VE"])
            eng.tensor_copy(dst, psum4[0:4, :]).then_inc(s.inc("VE", 1), 1)
    p.sec("vector", v2)
    return s.val("VE")


def emit_t4(p, src4, dstT, psum, nchunk=16):
    s = p.s
    ve_in = s.val("VE")
    se_in = s.val("SE")

    def te(eng):
        eng.wait_ge(s.h("VE"), ve_in)
        eng.wait_ge(s.h("SE"), se_in)
        for c in range(nchunk):
            mm = eng.matmul(psum[:, c * 4:(c + 1) * 4], src4[:, c * 128:(c + 1) * 128],
                            p.CONST[0:4, CI_IDENT, 0:4], start=True, stop=True)
        mm.then_inc(s.inc("TE", 1), 1)
    p.sec("tensor", te)
    t = s.val("TE")

    def sc(eng):
        eng.wait_ge(s.h("TE"), t)
        eng.activation(dstT, psum[:, 0:4 * nchunk], AF.Copy).then_inc(s.inc("SE", 1), 1)
    p.sec("scalar", sc)
    return s.val("SE")


def emit_small_proj(p, lhsT_ap, w_dram, w_cols, out4, psum, bias_rep4, nchunk=16):
    s = p.s
    nb = w_cols // 128
    wv = p.SCR[:, 0:2048].rearrange("p (c f) -> p c f", f=128)
    se_pre = s.val("SE")
    ve_pre = s.val("VE")
    for b_ in range(nb):
        te_pre = s.val("TE")

        def g(eng, b_=b_, te_pre=te_pre):
            if te_pre > 0:
                eng.wait_ge(s.h("TE"), te_pre)
            p.ld_dma(eng, wv[:],
                     w_dram[:, b_ * 128:(b_ + 1) * 128].rearrange(
                         "(c p) f -> p c f", p=128)[:, 0:nchunk, :])
        p.sec("gpsimd", g)
        ld1 = s.val("LD")

        def te(eng, b_=b_, ld1=ld1):
            eng.wait_ge(s.h("LD"), ld1)
            if b_ == 0:
                eng.wait_ge(s.h("SE"), se_pre)
                eng.wait_ge(s.h("VE"), ve_pre)
            for gc in range(nchunk):
                mm = eng.matmul(psum[0:4, b_ * 128:(b_ + 1) * 128], lhsT_ap(gc),
                                wv[:, gc, :], start=(gc == 0), stop=(gc == nchunk - 1))
            mm.then_inc(s.inc("TE", 1), 1)
        p.sec("tensor", te)
    t = s.val("TE")

    def v(eng):
        eng.wait_ge(s.h("TE"), t)
        eng.tensor_tensor(psum[0:4, 0:w_cols], psum[0:4, 0:w_cols], bias_rep4,
                          ALU.add).then_inc(s.inc("VE", 1), 1)
    p.sec("vector", v)
    t2 = s.val("VE")

    def sc(eng):
        eng.wait_ge(s.h("VE"), t2)
        eng.activation(out4, psum[0:4, 0:w_cols], AF.Copy).then_inc(s.inc("SE", 1), 1)
    p.sec("scalar", sc)
    return s.val("SE")


def d1_ap(p, gc):
    return p.P2[:, gc // 2, (gc % 2) * 1024:(gc % 2) * 1024 + 1024]


def d2_ap(p, gc):
    return p.P2[:, 8 + gc, :]


def ph_ffn(p, idx):
    s = p.s
    nc = p.nc

    ve_pre = s.val("VE")

    def w(eng):
        eng.wait_ge(s.h("VE"), ve_pre)
        eng.wait_ge(s.h("TE"), s.val("TE"))
        eng.wait_ge(s.h("SE"), s.val("SE"))
    t_ln = emit_ln_plain(p, lambda i: p.X[:, i, :], lambda i: p.P2[:, i, :],
                         LN_EPS_OUTER, wait=w)
    t_lnT = emit_transpose16(
        p, lambda i: p.P2[:, i, :],
        lambda i: p.P1[:, 0:8, i * 128:(i + 1) * 128],
        lambda i: p.P1[:, 8:16, i * 128:(i + 1) * 128], t_ln)

    def g1(eng):
        eng.wait_ge(s.h("VE"), t_ln)
        eng.wait_ge(s.h("LD"), s.val("LD"))
        eng.dma_start(out=p.cc["row_i"], in_=p.P2[127:128, 15, :]
                      ).then_inc(s.inc("LD", 16), 16)
        eng.wait_ge(s.h("LD"), s.val("LD"))
        eng.collective_compute("AllGather", ALU.bypass, replica_groups=p.RG8,
                               ins=[p.cc["row_i"]], outs=[p.cc["row_o"]]
                               ).then_inc(s.inc("CC", 1), 1)
    p.sec("gpsimd", g1)
    cc_ag = s.val("CC")

    def g2(eng):
        eng.wait_ge(s.h("CC"), cc_ag)
        p.ld_dma(eng, _A(p, 0), p.cc["row_o"][1:8:2, :])
        for r in range(4):
            p.ld_dma(eng, p.SCR[32 + r:33 + r, 2048:4096],
                     p.dt["in_g"].rearrange("(a d) -> a d", a=1))
            p.ld_dma(eng, p.SCR[64 + r:65 + r, 2048:4096],
                     p.dt["in_b"].rearrange("(a d) -> a d", a=1))
        p.ld_dma(eng, p.SMF[:, 208:224],
                 p.dt["gate_b"].rearrange("(c p) -> p c", p=128),
                 allow_slow_non_contiguous=True)
        eng.wait_ge(s.h("TPS"), t_lnT)
        for gc in range(16):
            p.ld_dma(eng, d1_ap(p, gc), p.dt["d1_w"][gc * 128:(gc + 1) * 128, :])
        for gc in range(8):
            p.ld_dma(eng, d2_ap(p, gc), p.dt["d2_w"][gc * 128:(gc + 1) * 128, :])
    p.sec("gpsimd", g2)
    ld_const = s.val("LD")

    def vw(eng):
        eng.wait_ge(s.h("LD"), ld_const)
    p.sec("vector", vw)

    with nc.psum_tensor("pX%d" % idx, [4, 2048], F32) as pX:
        t_xn = emit_ln4(p, _A(p, 0), _A(p, 0), LN_EPS_F, _A(p, 1), _A(p, 2), pX)

    def gagb(eng):
        eng.wait_ge(s.h("VE"), t_xn)
        for r in range(4):
            p.ld_dma(eng, p.SCR[32 + r:33 + r, 2048:4096],
                     p.dt["agg_b"].rearrange("(a d) -> a d", a=1))
    p.sec("gpsimd", gagb)
    ld_agb = s.val("LD")

    def vagb(eng):
        eng.wait_ge(s.h("LD"), ld_agb)
    p.sec("vector", vagb)

    with (
        nc.psum_tensor("pT%d" % idx, [128, 64], F32) as pT,
        nc.psum_tensor("pS%d" % idx, [4, 2048], F32) as pS,
    ):
        t = emit_t4(p, _A(p, 0), _t4buf(p), pT)

        def wv(eng):
            eng.wait_ge(s.h("SE"), t)
        p.sec("vector", wv)
        xn4T = lambda gc: p.SCR[:, 4160 + gc * 4:4160 + (gc + 1) * 4]
        t_ve = emit_small_proj(p, xn4T, p.dt["agg_w"], 2048, _A(p, 3), pS,
                               _A(p, 1))

        def gsw(eng):
            eng.wait_ge(s.h("SE"), t_ve)
            for r in range(4):
                p.ld_dma(eng, p.SCR[32 + r:33 + r, 2048:4096],
                         p.dt["pn_g"].rearrange("(a d) -> a d", a=1))
                p.ld_dma(eng, p.SCR[64 + r:65 + r, 2048:4096],
                         p.dt["pn_b"].rearrange("(a d) -> a d", a=1))
                p.ld_dma(eng, p.SCR[r:r + 1, 0:1024],
                         p.dt["d1_b"].rearrange("(a d) -> a d", a=1))
                p.ld_dma(eng, p.SCR[32 + r:33 + r, 0:2048],
                         p.dt["d2_b"].rearrange("(a d) -> a d", a=1))
        p.sec("gpsimd", gsw)
        ld_sw = s.val("LD")

        def vsw(eng):
            eng.wait_ge(s.h("LD"), ld_sw)
        p.sec("vector", vsw)

        if FFN_STAGE < 2:
            return
        for step in range(4):
            def wv2(eng, t=t_ve):
                eng.wait_ge(s.h("SE"), t)
            p.sec("vector", wv2)
            emit_ln4(p, _A(p, 3), _A(p, 0), LN_EPS_F, _A(p, 1), _A(p, 2), pS)
            t = emit_t4(p, _A(p, 0), _t4buf(p), pT)
            tlnT = lambda gc: p.SCR[:, 4160 + gc * 4:4160 + (gc + 1) * 4]
            def teh(eng, t=t):
                eng.wait_ge(s.h("SE"), t)
                for hb in range(2):
                    for gc in range(16):
                        mm = eng.matmul(pS[0:4, hb * 512:(hb + 1) * 512], tlnT(gc),
                                        d1_ap(p, gc)[:, hb * 512:(hb + 1) * 512],
                                        start=(gc == 0), stop=(gc == 15))
                mm.then_inc(s.inc("TE", 1), 1)
            p.sec("tensor", teh)
            t_h = s.val("TE")

            def sch(eng):
                eng.wait_ge(s.h("TE"), t_h)
                eng.activation(_A(p, 0, w=1024), pS[0:4, 0:1024],
                               AF.Copy).then_inc(s.inc("SE", 1), 1)
            p.sec("scalar", sch)
            t_g0 = s.val("SE")

            def vh1(eng):
                eng.wait_ge(s.h("SE"), t_g0)
                eng.tensor_tensor(_A(p, 0, w=1024), _A(p, 0, w=1024),
                                  _B(p, 0, w=1024), ALU.add
                                  ).then_inc(s.inc("VE", 1), 1)
            p.sec("vector", vh1)
            t_vh = s.val("VE")

            def sch2(eng):
                eng.wait_ge(s.h("VE"), t_vh)
                eng.activation(_A(p, 0, w=1024), _A(p, 0, w=1024),
                               AF.Gelu).then_inc(s.inc("SE", 1), 1)
            p.sec("scalar", sch2)
            t_g = s.val("SE")

            def wv3(eng):
                eng.wait_ge(s.h("SE"), t_g)
            p.sec("vector", wv3)
            t = emit_t4(p, _A(p, 0, w=1024), _t4buf(p, 32), pT, nchunk=8)
            hT = lambda gc: p.SCR[:, 4160 + gc * 4:4160 + (gc + 1) * 4]

            def ted(eng, t=t):
                eng.wait_ge(s.h("SE"), t)
                for fb in range(4):
                    for gc in range(8):
                        mm = eng.matmul(pS[0:4, fb * 512:(fb + 1) * 512], hT(gc),
                                        d2_ap(p, gc)[:, fb * 512:(fb + 1) * 512],
                                        start=(gc == 0), stop=(gc == 7))
                mm.then_inc(s.inc("TE", 1), 1)
            p.sec("tensor", ted)
            t_d = s.val("TE")

            def scd(eng):
                eng.wait_ge(s.h("TE"), t_d)
                eng.activation(_B(p, 3), pS[0:4, :], AF.Copy
                               ).then_inc(s.inc("SE", 1), 1)
            p.sec("scalar", scd)
            t_p = s.val("SE")

            def gar(eng):
                eng.wait_ge(s.h("SE"), t_p)
                eng.wait_ge(s.h("LD"), s.val("LD"))
                eng.dma_start(out=p.cc["th_i"], in_=_B(p, 3)
                              ).then_inc(s.inc("LD", 16), 16)
                eng.wait_ge(s.h("LD"), s.val("LD"))
                eng.collective_compute("AllReduce", ALU.add, replica_groups=p.RG8,
                                       ins=[p.cc["th_i"]], outs=[p.cc["th_o"]]
                                       ).then_inc(s.inc("CC", 1), 1)
                eng.wait_ge(s.h("CC"), s.val("CC"))
                p.ld_dma(eng, _B(p, 2), p.cc["th_o"])
            p.sec("gpsimd", gar)
            ld_ar = s.val("LD")

            def vup(eng):
                eng.wait_ge(s.h("LD"), ld_ar)
                eng.tensor_copy(pS[0:4, :], _B(p, 2)).then_inc(s.inc("VE", 1), 1)
                eng.wait_ge(s.h("VE"), s.count["VE"])
                eng.tensor_tensor(pS[0:4, :], pS[0:4, :], _B(p, 1),
                                  ALU.add).then_inc(s.inc("VE", 1), 1)
                eng.wait_ge(s.h("VE"), s.count["VE"])
                eng.tensor_tensor(_A(p, 3), _A(p, 3), pS[0:4, :],
                                  ALU.add).then_inc(s.inc("VE", 1), 1)
            p.sec("vector", vup)
            t = 0
            se_now = s.val("SE")
            ve_now = s.val("VE")

            def wsync(eng):
                eng.wait_ge(s.h("VE"), ve_now)
            p.sec("scalar", wsync)
            t = se_now

        if FFN_STAGE < 3:
            return
        def wv4(eng):
            eng.wait_ge(s.h("VE"), s.val("VE"))
            eng.tensor_copy(_A(p, 0), _A(p, 3)).then_inc(s.inc("VE", 1), 1)
        p.sec("vector", wv4)

        def gbcb(eng):
            eng.wait_ge(s.h("VE"), s.val("VE"))
            for r in range(4):
                p.ld_dma(eng, p.SCR[32 + r:33 + r, 2048:4096],
                         p.dt["bc_b"].rearrange("(a d) -> a d", a=1))
        p.sec("gpsimd", gbcb)
        ld_bcb = s.val("LD")

        def vbcb(eng):
            eng.wait_ge(s.h("LD"), ld_bcb)
        p.sec("vector", vbcb)

        t = emit_t4(p, _A(p, 0), _t4buf(p), pT)

        def wv5(eng):
            eng.wait_ge(s.h("SE"), t)
        p.sec("vector", wv5)
        thT = lambda gc: p.SCR[:, 4160 + gc * 4:4160 + (gc + 1) * 4]
        t_bc = emit_small_proj(p, thT, p.dt["bc_w"], 2048, _A(p, 2), pS,
                               _A(p, 1))

        def sel2(eng):
            for arm in eng.switch_core_id(n=128):
                b = (arm.logical % 8) // 2
                eng.wait_ge(s.h("SE"), t_bc)
                eng.wait_ge(s.h("EXC"), s.val("EXC"))
                eng.dma_start(out=p.SCR[0:1, 0:2048],
                              in_=p.SCR[64 + b:65 + b, 2048:4096]).then_inc(s.h("EXC"), 16)
        p.sec("sync", sel2)
        s.inc("EXC", 16)
        t_sel = s.val("EXC")

        def tebc(eng):
            eng.wait_ge(s.h("EXC"), t_sel)
            for c in range(16):
                mm = eng.matmul(pT[:, c:c + 1],
                                p.SCR[0:1, c * 128:(c + 1) * 128],
                                p.CONST[0:1, CI_ONES, 0:1], start=True, stop=True)
            mm.then_inc(s.inc("TE", 1), 1)
        p.sec("tensor", tebc)
        t_bcc = s.val("TE")

        def scbc(eng):
            eng.wait_ge(s.h("TE"), t_bcc)
            eng.activation(p.SMF[:, 192:208], pT[:, 0:16], AF.Copy
                           ).then_inc(s.inc("SE", 1), 1)
        p.sec("scalar", scbc)
        t_bccol = s.val("SE")

    if FFN_STAGE < 4:
        return
    with (
        nc.psum_tensor("pGA%d" % idx, [128, 2048], F32) as pGA,
        nc.psum_tensor("pGB%d" % idx, [128, 2048], F32) as pGB,
    ):
        ps = [pGA, pGB]
        sig = p.SCR[:, 2048:4096]
        te0 = s.val("TE")
        se0 = s.val("SE")
        ve0 = s.val("VE")
        mm_done = {fc: te0 + fc + 1 for fc in range(16)}
        sig_done = {fc: se0 + fc + 1 for fc in range(16)}
        cmb_done = {fc: ve0 + fc + 1 for fc in range(16)}

        def gw(eng):
            for fc in range(16):
                if fc >= 1:
                    eng.wait_ge(s.h("TE"), mm_done[fc - 1])
                else:
                    eng.wait_ge(s.h("SE"), t_bccol)
                p.ld_dma(eng, p.SCR[:, 0:2048].rearrange("p (c f) -> p c f", f=128),
                         p.dt["gate_w"][:, fc * 128:(fc + 1) * 128].rearrange(
                             "(c p) f -> p c f", p=128))
        p.sec("gpsimd", gw)
        gld = s.val("LD")

        def gte(eng):
            for fc in range(16):
                eng.wait_ge(s.h("LD"), gld - 16 * (15 - fc))
                if fc == 0:
                    eng.wait_ge(s.h("SE"), t_bccol)
                if fc >= 2:
                    eng.wait_ge(s.h("SE"), sig_done[fc - 2])
                for tq in range(4):
                    for gc in range(16):
                        mm = eng.matmul(ps[fc % 2][:, tq * 512:(tq + 1) * 512],
                                        p.SCR[:, gc * 128:(gc + 1) * 128],
                                        p.P1[:, gc, tq * 512:(tq + 1) * 512],
                                        start=(gc == 0), stop=(gc == 15))
                mm.then_inc(s.inc("TE", 1), 1)
        p.sec("tensor", gte)

        def gsc(eng):
            for fc in range(16):
                eng.wait_ge(s.h("TE"), mm_done[fc])
                if fc >= 1:
                    eng.wait_ge(s.h("VE"), cmb_done[fc - 1])
                eng.activation(sig, ps[fc % 2][:], AF.Sigmoid,
                               bias=p.SMF[:, 208 + fc:209 + fc]
                               ).then_inc(s.inc("SE", 1), 1)
        p.sec("scalar", gsc)

        def gve(eng):
            for fc in range(16):
                eng.wait_ge(s.h("SE"), sig_done[fc])
                eng.scalar_tensor_tensor(p.P2[:, fc, :], sig,
                                         p.SMF[:, 192 + fc:193 + fc],
                                         p.P1[:, fc, :], ALU.mult,
                                         ALU.add).then_inc(s.inc("VE", 1), 1)
        p.sec("vector", gve)
    t_out = s.val("VE")

    if FFN_STAGE < 5:
        return
    def tpo(eng):
        eng.wait_ge(s.h("VE"), t_out)
        for fc in range(16):
            eng.dma_start(out=p.P1[:, 0:16, fc * 128:(fc + 1) * 128],
                          in_=p.P2[:, fc, :], transpose=True
                          ).then_inc(s.inc("TPS", 16), 16)
    p.sec("sync", tpo)
    t_tp2 = s.val("TPS")

    def g3(eng):
        eng.wait_ge(s.h("TPS"), t_tp2)
        p.ld_dma(eng, p.SCR[0:1, 0:2048], p.dt["out_g"].rearrange("(a d) -> a d", a=1))
    p.sec("gpsimd", g3)
    ld_ob = s.val("LD")

    with nc.psum_tensor("pR2%d" % idx, [128, 2048], F32) as pR2:
        def ter(eng):
            eng.wait_ge(s.h("LD"), ld_ob)
            for fs in range(4):
                mm = eng.matmul(pR2[:, fs * 512:(fs + 1) * 512],
                                p.CONST[0:1, CI_ONES, :],
                                p.SCR[0:1, fs * 512:(fs + 1) * 512],
                                start=True, stop=True)
            mm.then_inc(s.inc("TE", 1), 1)
        p.sec("tensor", ter)
        t_g1 = s.val("TE")

        def scr1(eng):
            eng.wait_ge(s.h("TE"), t_g1)
            eng.activation(p.SCR[:, 0:2048], pR2[:], AF.Copy).then_inc(s.inc("SE", 1), 1)
        p.sec("scalar", scr1)
        t_gr = s.val("SE")

        def g4(eng):
            eng.wait_ge(s.h("TE"), t_g1)
            p.ld_dma(eng, p.SCR[0:1, 2048:4096], p.dt["out_b"].rearrange("(a d) -> a d", a=1))
        p.sec("gpsimd", g4)
        ld_ob2 = s.val("LD")

        def ter2(eng):
            eng.wait_ge(s.h("SE"), t_gr)
            eng.wait_ge(s.h("LD"), ld_ob2)
            for fs in range(4):
                mm = eng.matmul(pR2[:, fs * 512:(fs + 1) * 512],
                                p.CONST[0:1, CI_ONES, :],
                                p.SCR[0:1, 2048 + fs * 512:2048 + (fs + 1) * 512],
                                start=True, stop=True)
            mm.then_inc(s.inc("TE", 1), 1)
        p.sec("tensor", ter2)
        t_g2 = s.val("TE")

        def scr2(eng):
            eng.wait_ge(s.h("TE"), t_g2)
            eng.activation(p.SCR[:, 2048:4096], pR2[:], AF.Copy).then_inc(s.inc("SE", 1), 1)
        p.sec("scalar", scr2)
        t_br = s.val("SE")

    def w2(eng):
        eng.wait_ge(s.h("TPS"), t_tp2)
        eng.wait_ge(s.h("SE"), t_br)
    t_ln2 = emit_ln_plain(p, lambda i: p.P1[:, i, :], lambda i: p.P1[:, i, :],
                          LN_EPS_F, wait=w2)

    grep_ = p.SCR[:, 0:2048]
    brep_ = p.SCR[:, 2048:4096]

    def vf(eng):
        eng.wait_ge(s.h("VE"), t_ln2)
        for i in range(NSTRIP):
            eng.tensor_tensor(p.P1[:, i, :], p.P1[:, i, :], grep_,
                              ALU.mult).then_inc(s.inc("VE", 1), 1)
            eng.wait_ge(s.h("VE"), s.count["VE"])
            eng.tensor_tensor(p.P1[:, i, :], p.P1[:, i, :], brep_,
                              ALU.add).then_inc(s.inc("VE", 1), 1)
            eng.wait_ge(s.h("VE"), s.count["VE"])
            eng.tensor_tensor(p.X[:, i, :], p.X[:, i, :], p.P1[:, i, :],
                              ALU.add).then_inc(s.inc("VE", 1), 1)
    p.sec("vector", vf)
    p.t_ffn = s.val("VE")




SM_SCALE = 1.0 / (DH ** 0.5)


def ph_attn_b(p):
    s = p.s
    nc = p.nc

    ve_pre = s.val("VE")

    def w(eng):
        eng.wait_ge(s.h("VE"), ve_pre)
        eng.wait_ge(s.h("TE"), s.val("TE"))
        eng.wait_ge(s.h("SE"), s.val("SE"))
    t_ln = emit_ln_plain(p, lambda i: p.X[:, i, :], lambda i: p.P2[:, i, :],
                         LN_EPS_OUTER, wait=w)
    p.t_lnT = emit_transpose16(
        p, lambda i: p.P2[:, i, :],
        lambda i: p.P1[:, 0:8, i * 128:(i + 1) * 128],
        lambda i: p.P1[:, 8:16, i * 128:(i + 1) * 128], t_ln)

    lnfm = lambda gc: p.P1[:, gc, :]
    with (
        nc.psum_tensor("psQA", [128, 2048], F32) as psA,
        nc.psum_tensor("psQB", [128, 2048], F32) as psB,
    ):
        emit_proj_to_shared(p, "WQT", lnfm, 0, p.sh["qkv"], False, psA, psB)
        emit_proj_to_shared(p, "WKT", lnfm, 1, p.sh["qkv"], False, psA, psB)
        emit_proj_to_shared(p, "WVT", lnfm, 2, p.sh["qkv"], False, psA, psB)

    bar = emit_barrier(p)

    kT_buf = lambda par: p.P1[:, 2 * par:2 * par + 2, :].rearrange("p a b -> p (a b)")
    qT_buf = lambda par: p.P1[:, 4 + 2 * par:6 + 2 * par, :].rearrange("p a b -> p (a b)")
    vh_buf = lambda par: p.P1[:, 8 + 3 * par:11 + 3 * par, :].rearrange(
        "p a b -> p (a b)")[:, 0:4608].rearrange("p (a b) -> p a b", b=144)
    zslot = lambda h: p.P2[:, 2 * h:2 * h + 2, :].rearrange("p a b -> p (a b)")

    def vone(eng):
        eng.wait_ge(s.h("CC"), bar)
        eng.memset(p.SMF[:, 300:301], -6.0)
        eng.memset(p.P1[:, 8:14, :], 1.0).then_inc(s.inc("VE", 1), 1)
    p.sec("vector", vone)
    t_ones = s.val("VE")

    with (
        nc.psum_tensor("sA_b", [128, 1024], F32) as sA,
        nc.psum_tensor("sB_b", [128, 1024], F32) as sB,
        nc.psum_tensor("zAcc", [128, 2048], F32) as zAcc,
    ):
        sc_ps = [sA, sB]
        te_head = {}
        for h in range(8):
            par = h % 2
            te_now = s.val("TE")
            exc_after = s.val("EXC") + 16 * 6

            def rd(eng, parity, h=h, par=par, te_now=te_now, exc_after=exc_after):
                hg = 8 * parity + h
                eng.wait_ge(s.h("CC"), bar)
                eng.wait_ge(s.h("VE"), t_ones)
                if h >= 2:
                    eng.wait_ge(s.h("TE"), te_head[h - 2])
                elif te_now > 0:
                    eng.wait_ge(s.h("TE"), te_now)
                for wr in range(2):
                    eng.dma_start(out=p.P1[:, 2 * par + wr, :],
                                  in_=p.sh["qkv"][wr, 1, hg, :, :]
                                  ).then_inc(s.h("EXC"), 16)
                    eng.dma_start(out=p.P1[:, 4 + 2 * par + wr, :],
                                  in_=p.sh["qkv"][wr, 0, hg, :, :]
                                  ).then_inc(s.h("EXC"), 16)
                    eng.dma_start(out=vh_buf(par)[:, 16 * wr:16 * wr + 16, 0:128],
                                  in_=p.sh["qkv"][wr, 2, hg, :, :], transpose=True
                                  ).then_inc(s.h("EXC"), 16)
                eng.wait_ge(s.h("EXC"), exc_after)
            p.switch_sec("sync", rd)
            s.inc("EXC", 16 * 6)
            t_rd = s.val("EXC")
            emit_attnb_head(p, h, par, t_rd, kT_buf(par), qT_buf(par),
                            vh_buf(par), zslot(h), sc_ps, zAcc)
            te_head[h] = s.val("TE")

    emit_zb_collect_out(p)


def emit_attnb_head(p, h, par, t_rd, kT, qT, vh, zsl, sc_ps, zAcc):
    s = p.s
    RC = 308
    CELL = 4224

    for half in range(2):
        for b in range(4):
            qcs = [half * 16 + b * 4 + j for j in range(4)]
            ap = sc_ps[b % 2]
            ve_pre = s.val("VE")

            def tea(eng, qcs=qcs, ap=ap, ve_pre=ve_pre, half=half, b=b):
                if half == 0 and b == 0:
                    eng.wait_ge(s.h("EXC"), t_rd)
                eng.wait_ge(s.h("VE"), ve_pre)
                for j, qc in enumerate(qcs):
                    k0 = max(0, qc * 128 - 128)
                    c0 = j * 256 + (128 if qc == 0 else 0)
                    mm = eng.matmul(ap[:, c0:j * 256 + 256],
                                    qT[:, qc * 128:(qc + 1) * 128],
                                    kT[:, k0:qc * 128 + 128],
                                    start=(j % 2 == 0), stop=True,
                                    skip_group_check=True)
                mm.then_inc(s.inc("TE", 1), 1)
            p.sec("tensor", tea)
            t_mm = s.val("TE")

            def vea(eng, qcs=qcs, ap=ap, b=b, t_mm=t_mm):
                eng.wait_ge(s.h("TE"), t_mm)
                for j, qc in enumerate(qcs):
                    eng.tensor_tensor(ap[:, j * 256 + 128:j * 256 + 256],
                                      ap[:, j * 256 + 128:j * 256 + 256],
                                      p.CONST[:, CI_CADD, :],
                                      ALU.add).then_inc(s.inc("VE", 1), 1)
                eng.wait_ge(s.h("VE"), s.count["VE"])
                for j, qc in enumerate(qcs):
                    c0 = j * 256 + (128 if qc == 0 else 0)
                    cc = CELL + (b % 2) * 4 + j
                    eng.reduce_max(p.SCR[:, cc:cc + 1], ap[:, c0:j * 256 + 256],
                                   axis=mybir.AxisListType.X
                                   ).then_inc(s.inc("VE", 1), 1)
            p.sec("vector", vea)
            t_red = s.val("VE")

            def tet(eng, qcs=qcs, b=b, half=half, t_red=t_red):
                eng.wait_ge(s.h("VE"), t_red)
                for j, qc in enumerate(qcs):
                    seg = qc - half * 16
                    cc = CELL + (b % 2) * 4 + j
                    mm = eng.matmul(zAcc[0:1, seg * 128:(seg + 1) * 128],
                                    p.SCR[:, cc:cc + 1],
                                    p.CONST[:, CI_IDENT, :],
                                    start=(j == 0), stop=True,
                                    skip_group_check=True)
                mm.then_inc(s.inc("TE", 1), 1)
            p.sec("tensor", tet)

        t_tp = s.val("TE")

        def ver(eng, half=half, t_tp=t_tp):
            eng.wait_ge(s.h("TE"), t_tp)
            eng.tensor_scalar(zsl[0:1, half * 2048:(half + 1) * 2048],
                              zAcc[0:1, 0:2048], 0.0, None,
                              ALU.add).then_inc(s.inc("VE", 1), 1)
        p.sec("vector", ver)

    t_bc = s.val("VE")

    PT = [p.SCR[:, 0:2048], p.SCR[:, 2048:4096]]
    last_exp = {0: None, 1: None}
    last_pv = {0: None, 1: None}
    prev_ev = None
    for sp in range(4):
        span0 = sp * 1024
        nmc = sp * 8 + 8
        for mc in range(nmc):
            vstart = max(0, mc * 128 - span0)
            pp = sc_ps[mc % 2]
            le = last_exp[mc % 2]
            lp = last_pv[mc % 2]

            def tes(eng, mc=mc, span0=span0, vstart=vstart, pp=pp, sp=sp,
                    t_bc=t_bc, le=le):
                if sp == 0 and mc < 2:
                    eng.wait_ge(s.h("VE"), t_bc)
                if le is not None:
                    eng.wait_ge(s.h("SE"), le)
                for half in range(2):
                    if (half + 1) * 512 <= vstart:
                        continue
                    eng.matmul(pp[:, half * 512:(half + 1) * 512],
                               kT[:, mc * 128:(mc + 1) * 128],
                               qT[:, span0 + half * 512:span0 + (half + 1) * 512],
                               start=True, stop=False)
                    eng.matmul(pp[:, half * 512:(half + 1) * 512],
                               p.CONST[0:1, CI_MONES, 0:128],
                               zsl[0:1, span0 + half * 512:span0 + (half + 1) * 512],
                               start=False, stop=True).then_inc(s.inc("TE", 1), 1)
            p.sec("tensor", tes)
            t_mm = s.val("TE")

            def scs(eng, mc=mc, vstart=vstart, pp=pp, t_mm=t_mm, lp=lp):
                eng.wait_ge(s.h("TE"), t_mm)
                if lp is not None:
                    eng.wait_ge(s.h("TE"), lp)
                eng.activation(PT[mc % 2][:, vstart:1024], pp[:, vstart:1024],
                               AF.Exp, bias=p.SMF[:, 300:301],
                               scale=SM_SCALE).then_inc(s.inc("SE", 1), 1)
            p.sec("scalar", scs)
            last_exp[mc % 2] = s.val("SE")
            t_exp = s.val("SE")

            dq = mc * 128 - span0
            t_pt = s.val("VE")
            if 0 <= dq < 1024:
                def vm(eng, mc=mc, dq=dq, t_exp=t_exp):
                    eng.wait_ge(s.h("SE"), t_exp)
                    eng.tensor_tensor(PT[mc % 2][:, dq:dq + 128],
                                      PT[mc % 2][:, dq:dq + 128],
                                      p.CONST[:, CI_CAUSAL, :],
                                      ALU.mult).then_inc(s.inc("VE", 1), 1)
                p.sec("vector", vm)
                t_pt = s.val("VE")

            def tev(eng, mc=mc, sp=sp, vstart=vstart, t_exp=t_exp, t_pt=t_pt,
                    prev_ev=prev_ev):
                eng.wait_ge(s.h("SE"), t_exp)
                eng.wait_ge(s.h("VE"), t_pt)
                if mc == 0 and prev_ev is not None:
                    eng.wait_ge(s.h("VE"), prev_ev)
                for j in range(vstart // 128, 8):
                    mm = eng.matmul(zAcc[:, j * 256:j * 256 + 132],
                                    PT[mc % 2][:, j * 128:(j + 1) * 128],
                                    vh[:, mc, 0:132],
                                    start=(mc == 0 and j % 2 == 0),
                                    stop=(mc == 8 * sp + j),
                                    skip_group_check=True)
                mm.then_inc(s.inc("TE", 1), 1)
            p.sec("tensor", tev)
            last_pv[mc % 2] = s.val("TE")

        t_pv = s.val("TE")

        def vz(eng, sp=sp, t_pv=t_pv):
            eng.wait_ge(s.h("TE"), t_pv)
            for j in range(8):
                eng.wait_ge(s.h("VE"), s.count["VE"])
                eng.reciprocal(p.SMF[:, RC:RC + 1],
                               zAcc[:, j * 256 + 128:j * 256 + 129]
                               ).then_inc(s.inc("VE", 1), 1)
                eng.wait_ge(s.h("VE"), s.count["VE"])
                eng.tensor_scalar(zsl[:, (sp * 8 + j) * 128:(sp * 8 + j + 1) * 128],
                                  zAcc[:, j * 256:j * 256 + 128],
                                  p.SMF[:, RC:RC + 1], None,
                                  ALU.mult).then_inc(s.inc("VE", 1), 1)
        p.sec("vector", vz)
        prev_ev = s.val("VE")


def emit_zb_collect_out(p):
    s = p.s
    nc = p.nc
    ve_done = s.val("VE")

    def wr(eng, parity):
        eng.wait_ge(s.h("VE"), ve_done)
        for h in range(8):
            hg = 8 * parity + h
            eng.dma_start(out=p.sh["zb"][hg, 0:16].rearrange("j t e -> t j e"),
                          in_=p.P2[:, 2 * h, :].rearrange("p (j e) -> p j e", e=128)
                          ).then_inc(s.h("EXC"), 16)
            eng.dma_start(out=p.sh["zb"][hg, 16:32].rearrange("j t e -> t j e"),
                          in_=p.P2[:, 2 * h + 1, :].rearrange("p (j e) -> p j e", e=128)
                          ).then_inc(s.h("EXC"), 16)
    p.switch_sec("sync", wr)
    s.inc("EXC", 16 * 16)

    bar = emit_barrier(p)

    def rd(eng, parity):
        eng.wait_ge(s.h("CC"), bar)
        J0 = 16 * parity
        for hg in range(16):
            eng.dma_start(
                out=p.P1[:, 0:8, hg * 128:(hg + 1) * 128],
                in_=p.sh["zb"][hg, J0:J0 + 8].rearrange("j t e -> t j e")
                ).then_inc(s.h("EXC"), 16)
            eng.dma_start(
                out=p.P1[:, 8:16, hg * 128:(hg + 1) * 128],
                in_=p.sh["zb"][hg, J0 + 8:J0 + 16].rearrange("j t e -> t j e")
                ).then_inc(s.h("EXC"), 16)
    p.switch_sec("sync", rd)
    s.inc("EXC", 16 * 32)
    t_zread = s.val("EXC")

    def tp(eng):
        eng.wait_ge(s.h("EXC"), t_zread)
        for i in range(NSTRIP):
            eng.dma_start(out=p.P2[:, 0:8, i * 128:(i + 1) * 128],
                          in_=p.P1[:, i, 0:1024],
                          transpose=True).then_inc(s.inc("TPS", 16), 16)
            eng.dma_start(out=p.P2[:, 8:16, i * 128:(i + 1) * 128],
                          in_=p.P1[:, i, 1024:2048], transpose=True
                          ).then_inc(s.inc("TPS", 16), 16)
    p.sec("sync", tp)
    t_tp = s.val("TPS")

    def g(eng):
        eng.wait_ge(s.h("TPS"), t_tp)
        for gc in range(16):
            p.ld_dma(eng, p.P1[:, gc, :].rearrange("p (c f) -> p c f", f=512),
                     p.dt["WOT"][gc * 128:(gc + 1) * 128, :].rearrange(
                         "p (c f) -> p c f", f=512))
    p.sec("gpsimd", g)
    ld_wo = s.val("LD")

    with (
        nc.psum_tensor("psOA", [128, 2048], F32) as psA,
        nc.psum_tensor("psOB", [128, 2048], F32) as psB,
    ):
        te0 = s.val("TE")
        ve0 = s.val("VE")
        te_ticks = {tt: te0 + tt + 1 for tt in range(16)}

        def te2(eng):
            eng.wait_ge(s.h("LD"), ld_wo)
            for tt in range(16):
                pp = psA if tt % 2 == 0 else psB
                if tt >= 2:
                    eng.wait_ge(s.h("VE"), ve0 + tt - 1)
                for gc in range(16):
                    lhs = p.P2[:, gc, tt * 128:(tt + 1) * 128]
                    for fs in range(4):
                        mm = eng.matmul(pp[:, fs * 512:(fs + 1) * 512], lhs,
                                        p.P1[:, gc, fs * 512:(fs + 1) * 512],
                                        start=(gc == 0), stop=(gc == 15))
                mm.then_inc(s.inc("TE", 1), 1)
        p.sec("tensor", te2)

        def v2(eng):
            for tt in range(16):
                pp = psA if tt % 2 == 0 else psB
                eng.wait_ge(s.h("TE"), te_ticks[tt])
                eng.tensor_tensor(p.X[:, tt, :], pp[:], p.X[:, tt, :],
                                  ALU.add).then_inc(s.inc("VE", 1), 1)
        p.sec("vector", v2)
    p.t_x3 = s.val("VE")
